# version 01
# baseline (speedup 1.0000x reference)
import os
from contextlib import ExitStack

import numpy as np
import ml_dtypes

import concourse.bass as bass
import concourse.mybir as mybir
from concourse.bass_utils import run_bass_kernel_spmd
from concourse.bass_core_rust import IndirectOffsetOnAxis

F32 = mybir.dt.float32
BF16 = mybir.dt.bfloat16
U32 = mybir.dt.uint32
AF = mybir.ActivationFunctionType
ALU = mybir.AluOpType
AX = mybir.AxisListType
AP = bass.AP

NCORES = 8
D = 1024
TLOC = 4096
NT = 32
NE = 16
CPAD = 576
CSL = [(0, 128), (128, 128), (256, 128), (384, 128), (512, 64)]
NBIS = 26
SLAB = 512
NSLAB = TLOC // SLAB
S = 2048
NH = 16
ATT_SCALE = 1.0 / (64 * 64 * 16)
BIGIOTA = 65536.0

ENGINES = ("sync", "scalar", "tensor", "vector", "gpsimd")


class Sem:
    __slots__ = ("h", "n", "name")

    def __init__(self, ctx, nc, name):
        self.h = ctx.enter_context(nc.semaphore(name))
        self.n = 0
        self.name = name


def _ap_tname(a):
    try:
        return a.tensor.name
    except Exception:
        return None


class K:
    def __init__(self, nc, ctx):
        self.nc = nc
        self.ctx = ctx
        self.streams = {e: [] for e in ENGINES}
        self.sems = {}
        self.t = {}
        self.bar_k = 0
        self.dirty = {e: set() for e in ENGINES}
        self.n_drain = 0

    def sem(self, name):
        if name not in self.sems:
            self.sems[name] = Sem(self.ctx, self.nc, name)
        return self.sems[name]

    def em(self, eng, fn):
        self.streams[eng].append(fn)

    def dma(self, eng, out, in_, sem, **kw):
        sem.n += 16
        self.em(eng, lambda E, o=out, i=in_, s=sem.h, kw=dict(kw): E.dma_start(out=o, in_=i, **kw).then_inc(s, 16))
        return sem.n

    def wait(self, eng, sem, val):
        self.em(eng, lambda E, s=sem.h, v=val: E.wait_ge(s, v))

    def drain(self, eng):
        self.em(eng, lambda E: E.drain())
        self.dirty[eng].clear()
        self.n_drain += 1

    def op(self, eng, meth, *args, sem=None, extra_incs=(), allow_lp=False, **kw):
        if eng in ("vector", "scalar", "gpsimd"):
            ins = [a for a in args[1:] if isinstance(a, AP)]
            for key in ("bias", "scale"):
                if isinstance(kw.get(key), AP):
                    ins.append(kw[key])
            innames = {_ap_tname(a) for a in ins}
            if innames & self.dirty[eng]:
                self.drain(eng)
            out_name = _ap_tname(args[0])
            if out_name:
                self.dirty[eng].add(out_name)
        v = None
        if sem is not None:
            sem.n += 1
            v = sem.n
        for s2 in extra_incs:
            s2.n += 1
        def fn(E, m=meth, a=args, kw=dict(kw), s=sem.h if sem else None,
               xs=tuple(s2.h for s2 in extra_incs), lp=allow_lp):
            if lp:
                with E.bass.allow_low_precision(reason="small exact integers"):
                    ins = getattr(E, m)(*a, **kw)
            else:
                ins = getattr(E, m)(*a, **kw)
            if s is not None:
                ins.then_inc(s, 1)
            for s2 in xs:
                ins.then_inc(s2, 1)
        self.em(eng, fn)
        return v

    def sem_inc(self, eng, sem):
        sem.n += 1
        self.em(eng, lambda E, s=sem.h: E.sem_inc(s, 1))
        return sem.n


class SA:
    def __init__(self, nc):
        self.nc = nc
        self.limit = nc.SBUF_PARTITION_SIZE_BYTES
        base = 0
        if nc.dma_scratch is not None:
            ml = nc.lookup_mloc(nc.dma_scratch)
            base = (ml.addr + 16384 + 127) // 128 * 128
        self.off = base
        self.union_base = None
        self.peak = 0

    @staticmethod
    def _per(shape, dtype):
        per = int(np.prod(shape[1:])) * mybir.dt.size(dtype)
        return (per + 31) // 32 * 32

    def pers(self, name, shape, dtype):
        ap = self.nc.alloc_sbuf_tensor_at(name, list(shape), dtype, offset=self.off).ap()
        self.off += self._per(shape, dtype)
        assert self.off <= self.limit, f"SBUF pers overflow at {name}: {self.off}"
        return ap

    def close(self):
        self.union_base = self.off

    def union(self, name, shape, dtype, uoff):
        off = self.union_base + uoff
        end = off + self._per(shape, dtype)
        self.peak = max(self.peak, end)
        assert end <= self.limit, f"SBUF union overflow at {name}: {end} > {self.limit}"
        return self.nc.alloc_sbuf_tensor_at(name, list(shape), dtype, offset=off).ap()


def emit_ln(k, src, tag, pre=None):
    t = k.t
    s_xld = k.sem("xld")
    s_mv = k.sem("mv")
    s_std = k.sem("std")
    s_xlnv = k.sem("xlnv")
    s_xlnst = [k.sem("xlnst0"), k.sem("xlnst1")]
    ln_base = (s_xlnst[0].n, s_xlnst[1].n)
    if pre is not None:
        for sem, val in pre:
            k.wait("sync", sem, val)
    free_at = {}
    store_at = {}
    for tt in range(NT):
        par = tt % 2
        xt = t["xt"][par]
        xlt = t["xlt"][par]
        if par in free_at:
            k.wait("sync", s_xlnv, free_at[par])
        vld = k.dma("sync", xt[:, :], src[tt * 128:(tt + 1) * 128, :], s_xld)
        k.wait("vector", s_xld, vld)
        k.op("vector", "bn_stats", t["bst"][:, 0:6], xt[:, 0:512])
        k.op("vector", "bn_stats", t["bst"][:, 6:12], xt[:, 512:1024])
        vmv = k.op("vector", "bn_aggr", t["mv"][:, :], t["bst"][:, 0:12], sem=s_mv)
        k.wait("scalar", s_mv, vmv)
        vstd = k.op("scalar", "activation", t["stdv"][:, :], t["mv"][:, 1:2], sem=s_std, func=AF.Sqrt, bias=t["epsc"][:, :])
        k.wait("vector", s_std, vstd)
        k.op("vector", "reciprocal", t["rstd"][:, :], t["stdv"][:, :])
        k.op("vector", "tensor_mul", t["nmr"][:, :], t["mv"][:, 0:1], t["rstd"][:, :])
        if par in store_at:
            k.wait("vector", s_xlnst[par], store_at[par])
        vxln = k.op("vector", "tensor_scalar", xlt[:, :], xt[:, :], t["rstd"][:, :], t["nmr"][:, :],
                    sem=s_xlnv, op0=ALU.mult, op1=ALU.subtract)
        free_at[par] = vxln
        k.wait("sync", s_xlnv, vxln)
        store_at[par] = k.dma("sync", t["xln"][tt * 128:(tt + 1) * 128, :], xlt[:, :], s_xlnst[par])
    return s_xlnst, ln_base


def emit_gate_shared(k, app, src, xmid, gate, ln_base):
    t = k.t
    s_xlnst = [k.sem("xlnst0"), k.sem("xlnst1")]
    s_xT = k.sem("xT")
    s_gmm = k.sem("gmm")
    s_gact = k.sem("gact")
    s_gsm = k.sem("gsm")
    s_scst = k.sem("scst")
    s_upmm = k.sem("upmm")
    s_gelu = k.sem("gelu")
    s_hsh = k.sem("hsh")
    s_dnmm = k.sem("dnmm")
    s_x2 = k.sem("x2")
    s_x1v = k.sem("x1v")
    s_x1st = [k.sem("x1st0"), k.sem("x1st1")]
    s_wld = k.sem("wldB")

    if gate is not None:
        gs, gv = gate
        k.wait("gpsimd", gs, gv)
    s_wldd = k.sem("wldD")
    k.dma("gpsimd", t["sgw"][:, :, :], t["sgt"].rearrange("(dc p) i -> p dc i", p=128), s_wld)
    k.dma("gpsimd", t["suw"][:, :, :], t["sut"].rearrange("(dc p) i -> p dc i", p=128), s_wld)
    wld_up = s_wld.n
    k.dma("gpsimd", t["sdw"][:, :, :], t["sdt"].rearrange("(ic p) d -> p ic d", p=128), s_wldd)
    wld_dn = s_wldd.n
    k.wait("tensor", s_wld, wld_up)

    xT_free = {}
    x1_st = {}
    pair_free = {}
    dn_free = {}
    gate_ps_free = None
    for sl in range(NSLAB):
        spar = sl % 2
        xlnT = t["xlnT"][spar]
        if spar in xT_free:
            k.wait("sync", s_upmm, xT_free[spar])
        k.wait("sync", s_xlnst[0], ln_base[0] + 16 * 2 * (sl + 1))
        k.wait("sync", s_xlnst[1], ln_base[1] + 16 * 2 * (sl + 1))
        for dc in range(8):
            k.dma("sync", xlnT[:, dc, :],
                  t["xln"][sl * SLAB:(sl + 1) * SLAB, dc * 128:(dc + 1) * 128], s_xT, transpose=True)
        vxT = s_xT.n
        k.wait("tensor", s_xT, vxT)
        for ts in ([] if os.environ.get("KSKIP_GATE") else range(4)):
            tile = sl * 4 + ts
            if gate_ps_free is not None:
                k.wait("tensor", s_gact, gate_ps_free)
            for dc in range(8):
                k.op("tensor", "matmul", t["PS_C"][:, 0:16],
                     xlnT[:, dc, ts * 128:(ts + 1) * 128], t["gw"][:, dc, :],
                     sem=(s_gmm if dc == 7 else None), start=(dc == 0), stop=(dc == 7))
            k.wait("vector", s_gmm, s_gmm.n)
            k.op("vector", "tensor_reduce", t["gmax"][:, :], t["PS_C"][:, 0:16], AX.X, ALU.max)
            vneg = k.op("vector", "tensor_scalar_mul", t["gneg"][:, :], t["gmax"][:, :], -1.0, sem=s_gsm)
            k.wait("scalar", s_gsm, vneg)
            gate_ps_free = k.op("scalar", "activation", t["gexp"][:, :], t["PS_C"][:, 0:16],
                                sem=s_gact, func=AF.Exp, bias=t["gneg"][:, :])
            k.wait("vector", s_gact, gate_ps_free)
            k.op("vector", "reduce_sum", t["gsum"][:, :], t["gexp"][:, :], AX.X)
            k.op("vector", "reciprocal", t["grec"][:, :], t["gsum"][:, :])
            vsc = k.op("vector", "tensor_scalar_mul",
                       t["scores"][:, tile * 16:(tile + 1) * 16], t["gexp"][:, :], t["grec"][:, :], sem=s_gsm)
            k.wait("sync", s_gsm, vsc)
            k.dma("sync", t["sc_loc"][tile * 128:(tile + 1) * 128, :],
                  t["scores"][:, tile * 16:(tile + 1) * 16], s_scst)
        for ist in range(16):
            gi = sl * 16 + ist
            qq = gi % 2
            sgp = t["PS_A"][:, qq * 1024: qq * 1024 + 512]
            sup = t["PS_A"][:, qq * 1024 + 512: qq * 1024 + 1024]
            if qq in pair_free:
                k.wait("tensor", s_hsh, pair_free[qq])
            for dc in range(8):
                k.op("tensor", "matmul", sgp, t["sgw"][:, dc, ist * 128:(ist + 1) * 128],
                     xlnT[:, dc, :], start=(dc == 0), stop=(dc == 7))
            for dc in range(8):
                k.op("tensor", "matmul", sup, t["suw"][:, dc, ist * 128:(ist + 1) * 128],
                     xlnT[:, dc, :],
                     sem=(s_upmm if dc == 7 else None), start=(dc == 0), stop=(dc == 7))
            k.wait("scalar", s_upmm, s_upmm.n)
            vge = k.op("scalar", "activation", t["gsg"][qq][:, :], sgp, sem=s_gelu, func=AF.Gelu, bias=t["zeroc"][:, :])
            k.wait("vector", s_gelu, vge)
            pair_free[qq] = k.op("vector", "tensor_tensor", t["hsh"][:, ist, :], t["gsg"][qq][:, :], sup,
                                 sem=s_hsh, op=ALU.mult)
        k.wait("tensor", s_wldd, wld_dn)
        k.wait("tensor", s_hsh, s_hsh.n)
        for ts in ([] if os.environ.get("KSKIP_DOWN") else range(4)):
            tile = sl * 4 + ts
            tpar = tile % 2
            if tpar in x1_st:
                k.wait("sync", s_x1st[tpar], x1_st[tpar])
            vx2 = k.dma("sync", t["xt2"][tpar][:, :], src[tile * 128:(tile + 1) * 128, :], s_x2)
            for nch in range(2):
                rr = (tile * 2 + nch) % 2
                dn = t["PS_B"][:, rr * 512:(rr + 1) * 512]
                if rr in dn_free:
                    k.wait("tensor", s_x1v, dn_free[rr])
                for ic in range(16):
                    k.op("tensor", "matmul", dn,
                         t["hsh"][:, ic, ts * 128:(ts + 1) * 128],
                         t["sdw"][:, ic, nch * 512:(nch + 1) * 512],
                         sem=(s_dnmm if ic == 15 else None), start=(ic == 0), stop=(ic == 15))
                k.wait("vector", s_dnmm, s_dnmm.n)
                k.wait("vector", s_x2, vx2)
                dn_free[rr] = k.op("vector", "scalar_tensor_tensor",
                                   t["xacc"][tpar][:, nch * 512:(nch + 1) * 512], dn, 0.5,
                                   t["xt2"][tpar][:, nch * 512:(nch + 1) * 512],
                                   sem=s_x1v, op0=ALU.mult, op1=ALU.add)
            k.wait("sync", s_x1v, s_x1v.n)
            x1_st[tpar] = k.dma("sync", xmid[tile * 128:(tile + 1) * 128, :], t["xacc"][tpar][:, :], s_x1st[tpar])
        xT_free[spar] = s_upmm.n
    return (s_x1st[0].n, s_x1st[1].n)


def emit_routing(k, app):
    t = k.t
    s_scst = k.sem("scst")
    s_cc = k.sem("cc")
    s_scg = k.sem("scg")
    s_bv = k.sem("bv")
    s_bt = k.sem("bt")
    s_rk = k.sem("rk")

    k.wait("gpsimd", s_scst, s_scst.n)
    s_cc.n += 1
    def cc(E, i=t["sc_loc"], o=t["sc_glob"], s=s_cc.h):
        E.collective_compute("AllGather", ALU.bypass, replica_groups=[list(range(NCORES))],
                             ins=[i.tensor.ap().opt()], outs=[o.tensor.ap().opt()]).then_inc(s, 1)
    k.em("gpsimd", cc)
    k.wait("sync", s_cc, s_cc.n)
    vscg = k.dma("sync", t["scg"][:, :, :], t["sc_glob"].rearrange("(p q) e -> p q e", p=128), s_scg)
    k.wait("vector", s_scg, vscg)
    k.op("vector", "memset", t["theta"][:, :], 0.5)
    step = 0.25
    scv = t["scg"][:, :, :].rearrange("p q e -> p e q")
    thb = t["theta"][:, :].unsqueeze(-1).broadcast_to([128, 16, 256])
    cmpv = t["cmp"][:, :].rearrange("p (e q) -> p e q", e=16)
    for it in range(NBIS):
        k.op("vector", "tensor_tensor", cmpv, scv, thb, op=ALU.is_ge)
        vred = k.op("vector", "tensor_reduce", t["red"][:, :], cmpv, AX.X, ALU.add, sem=s_bv, allow_lp=True)
        k.wait("tensor", s_bv, vred)
        vcnt = k.op("tensor", "matmul", t["PS_C"][:, 16:32], t["c_ones128"][:, :], t["red"][:, :], sem=s_bt,
                    start=True, stop=True)
        k.wait("vector", s_bt, vcnt)
        k.op("vector", "tensor_single_scalar", t["dir"][:, :], t["PS_C"][:, 16:32], 4096.0, op=ALU.is_gt)
        k.op("vector", "scalar_tensor_tensor", t["theta"][:, :], t["dir"][:, :], 2.0 * step, t["theta"][:, :],
             op0=ALU.mult, op1=ALU.add)
        k.op("vector", "tensor_scalar_add", t["theta"][:, :], t["theta"][:, :], -step, sem=s_bv)
        step *= 0.5
    scb = t["scores"][:, :].rearrange("p (tt e) -> p tt e", e=16)
    thb2 = t["theta"][:, :].unsqueeze(1).broadcast_to([128, NT, 16])
    vm = k.op("vector", "tensor_tensor", t["mask"][:, :].rearrange("p (tt e) -> p tt e", e=16), scb, thb2,
              sem=s_bv, op=ALU.is_ge)
    k.wait("tensor", s_bv, vm)
    k.op("tensor", "matmul", t["PS_A"][:, 0:512], t["c_t1"][:, :], t["mask"][:, :], start=True, stop=True)
    for c in range(4):
        k.op("tensor", "matmul", t["PS_C"][:, 32 + c:33 + c], t["mask"][:, c * 128:(c + 1) * 128],
             t["c_ones_col"][:, :], sem=(s_bt if c == 3 else None), start=True, stop=True)
    k.wait("vector", s_bt, s_bt.n)
    vcs = k.op("vector", "tensor_copy", t["cs_bf"][:, :], t["PS_C"][:, 32:36], sem=s_bv)
    k.wait("tensor", s_bv, vcs)
    for c in range(4):
        for cp in range(c + 1):
            k.op("tensor", "matmul", t["PS_C"][:, 36 + c:37 + c],
                 t["c_tdiag"][:, :] if cp == c else t["c_tfull"][:, :],
                 t["cs_bf"][:, cp:cp + 1],
                 sem=(s_bt if (c == 3 and cp == 3) else None), start=(cp == 0), stop=(cp == c))
    k.wait("vector", s_bt, s_bt.n)
    voff = k.op("vector", "tensor_copy", t["off_f"][:, :], t["PS_C"][:, 36:40], sem=s_bv)
    s_rt = k.sem("rowt")
    k.wait("sync", s_bv, voff)
    vst = k.dma("sync", t["rowtmp"].rearrange("(c p) one -> p c one", p=128), t["off_f"][:, :].unsqueeze(-1), s_rt, allow_slow_non_contiguous=True)
    k.wait("sync", s_rt, vst)
    vld = k.dma("sync", t["orow"][0:1, :], t["rowtmp"].rearrange("q one -> one q"), s_rt)
    k.wait("tensor", s_rt, vld)
    vbc = k.op("tensor", "matmul", t["PS_A"][:, 1024:1536], t["ones_row"][0:1, :], t["orow"][0:1, :], sem=s_bt,
               start=True, stop=True)
    k.wait("vector", s_bt, vbc)
    k.op("vector", "tensor_copy", t["prefw"][:, :], t["PS_A"][:, 0:512])
    k.op("vector", "tensor_add", t["rank"][:, :], t["prefw"][:, :], t["PS_A"][:, 1024:1536])
    k.op("vector", "tensor_scalar", t["notm"][:, :], t["mask"][:, :], -1.0, 1.0, op0=ALU.mult, op1=ALU.add)
    k.op("vector", "scalar_tensor_tensor", t["idxf_f"][:, :], t["notm"][:, :], t["c_bigiota"][:, :],
         t["rank"][:, :], op0=ALU.mult, op1=ALU.add)
    k.op("vector", "tensor_copy", t["idxf"][:, :], t["idxf_f"][:, :])
    k.op("vector", "tensor_scalar_min", t["idxb_f"][:, :], t["rank"][:, :], float(CPAD - 1))
    k.op("vector", "tensor_copy", t["idxb"][:, :], t["idxb_f"][:, :])
    k.op("vector", "tensor_single_scalar", t["rok"][:, :], t["rank"][:, :], float(CPAD), op=ALU.is_lt)
    k.op("vector", "tensor_scalar_mul", t["weff"][:, :], t["scores"][:, :], 0.5)
    k.op("vector", "tensor_mul", t["weff"][:, :], t["weff"][:, :], t["mask"][:, :])
    vrk = k.op("vector", "tensor_mul", t["weff"][:, :], t["weff"][:, :], t["rok"][:, :], sem=s_rk)
    return vrk


def emit_fwd_scatter(k, app, rk_val):
    t = k.t
    s_rk = k.sem("rk")
    s_xr = [k.sem("xr0"), k.sem("xr1")]
    s_xrfree = k.sem("xrfree")
    s_fsc = [[k.sem(f"fsc{i}_0"), k.sem(f"fsc{i}_1")] for i in range(4)]
    s_zero = k.sem("zero")
    k.wait("gpsimd", s_rk, rk_val)
    k.wait("gpsimd", s_zero, s_zero.n)
    free_at = {}
    group_done = []
    prev_tile_sc = None
    for eg in range(4):
        for tt in range(NT):
            par = tt % 2
            if par in free_at:
                k.wait("sync", s_xrfree, free_at[par])
            vxr = k.dma("sync", t["xlt"][par][:, :], t["xln"][tt * 128:(tt + 1) * 128, :], s_xr[par])
            k.wait("gpsimd", s_xr[par], vxr)
            fsem = s_fsc[eg][par]
            for ei in range(4):
                e = eg * 4 + ei
                idx = t["idxf"][:, tt * 16 + e: tt * 16 + e + 1]
                fsem.n += 16
                def fsc(E, o=t[f"xg{e}"], ia=t["xlt"][par][:, :], ix=idx, s=fsem.h, _k=k):
                    E.indirect_dma_start(out=o[:, :], out_offset=IndirectOffsetOnAxis(ap=ix, axis=0),
                                         in_=ia, in_offset=None,
                                         bounds_check=_k.bc_reg, oob_is_err=False).then_inc(s, 16)
                k.em("gpsimd", fsc)
            if prev_tile_sc is not None:
                k.wait("gpsimd", prev_tile_sc[0], prev_tile_sc[1])
                free_at[1 - par] = k.sem_inc("gpsimd", s_xrfree)
            prev_tile_sc = (fsem, fsem.n)
        group_done.append([(s_fsc[eg][0], s_fsc[eg][0].n), (s_fsc[eg][1], s_fsc[eg][1].n)])
    return group_done


def emit_experts(k, app, group_done, rk_val, shared_done):
    t = k.t
    s_rk = k.sem("rk")
    s_x1v = k.sem("x1v")
    s_xgT = k.sem("xgT")
    s_xgTfree = k.sem("xgTfree")
    s_wgu = [k.sem("wgu0"), k.sem("wgu1")]
    s_wd = [k.sem("wdq0"), k.sem("wdq1")]
    s_up = k.sem("upF")
    s_gelu2 = k.sem("gelu2")
    s_h2 = k.sem("h2")
    s_dn = k.sem("dnF")
    s_eov = k.sem("eov")
    s_eost = [k.sem("eost0"), k.sem("eost1")]

    k.wait("gpsimd", s_x1v, shared_done)
    k.wait("tensor", s_rk, rk_val)

    wd_free_at = {}
    xgT_free = {}
    pair_free = {}
    dn_all_prev = None
    eo_rot_free = {}
    wgu_done = {}
    wd_done = {}
    eo_store_prev = {}
    for e in range(NE):
        ep = e % 2
        grp = e // 4
        for gsem, gval in group_done[grp]:
            k.wait("sync", gsem, gval)
        if ep in xgT_free:
            k.wait("sync", k.sem("upF"), xgT_free[ep])
        for dc in range(8):
            k.dma("sync", t["xgT"][ep][:, dc, :],
                  t[f"xg{e}"][:, dc * 128:(dc + 1) * 128], s_xgT, transpose=True)
        vxgT = s_xgT.n
        up_base_e = s_up.n
        for ig in range(8):
            gi = e * 8 + ig
            if gi >= 2:
                if ig >= 2:
                    k.wait("gpsimd", s_up, up_base_e + 8 * (ig - 1))
                else:
                    k.wait("gpsimd", s_up, up_base_e - 8 * (1 - ig))
            wp = gi % 2
            k.dma("gpsimd", t["wgb"][wp][:, :, :],
                  t["wgt"][e, :, ig * 512:(ig + 1) * 512].rearrange("(dc p) i -> p dc i", p=128), s_wgu[wp])
            k.dma("gpsimd", t["wub"][wp][:, :, :],
                  t["wut"][e, :, ig * 512:(ig + 1) * 512].rearrange("(dc p) i -> p dc i", p=128), s_wgu[wp])
            wgu_done[(e, ig)] = s_wgu[wp].n
        for hf in range(2):
            wi = e * 2 + hf
            if wi >= 2:
                k.wait("gpsimd", s_dn, wd_free_at[wi - 2])
            hp = wi % 2
            for q in range(4):
                k.dma("gpsimd", t["wdb"][hp][:, q * 8:(q + 1) * 8, :],
                      t["wdt"][e, q * 1024:(q + 1) * 1024, hf * 512:(hf + 1) * 512]
                      .rearrange("(ic p) d -> p ic d", p=128), s_wd[hp])
            wd_done[(e, hf)] = s_wd[hp].n
        k.wait("tensor", s_xgT, vxgT)
        if dn_all_prev is not None:
            k.wait("vector", s_dn, dn_all_prev)
        for ig in range(8):
            gi = e * 8 + ig
            wp = gi % 2
            k.wait("tensor", s_wgu[wp], wgu_done[(e, ig)])
            for it in range(4):
                itile = ig * 4 + it
                qq = itile % 2
                g5 = t["PS_A"][:, qq * 1024: qq * 1024 + 512]
                u5 = t["PS_A"][:, qq * 1024 + 512: qq * 1024 + 1024]
                if qq in pair_free:
                    k.wait("tensor", s_h2, pair_free[qq])
                for dc in range(8):
                    k.op("tensor", "matmul", g5, t["wgb"][wp][:, dc, it * 128:(it + 1) * 128],
                         t["xgT"][ep][:, dc, 0:512], start=(dc == 0), stop=(dc == 7))
                for dc in range(8):
                    k.op("tensor", "matmul", u5, t["wub"][wp][:, dc, it * 128:(it + 1) * 128],
                         t["xgT"][ep][:, dc, 0:512],
                         sem=(s_up if dc == 7 else None), start=(dc == 0), stop=(dc == 7))
                k.wait("scalar", s_up, s_up.n)
                vge = k.op("scalar", "activation", t["gsg2"][qq][:, 0:512], g5, sem=s_gelu2, func=AF.Gelu, bias=t["zeroc"][:, :])
                k.wait("vector", s_gelu2, vge)
                pair_free[qq] = k.op("vector", "tensor_tensor", t["hsb"][:, itile, 0:512],
                                     t["gsg2"][qq][:, 0:512], u5, sem=s_h2, op=ALU.mult)
            for it in range(4):
                itile = ig * 4 + it
                qq = itile % 2
                g6 = t["PS_A"][:, qq * 1024: qq * 1024 + 64]
                u6 = t["PS_A"][:, qq * 1024 + 512: qq * 1024 + 512 + 64]
                k.wait("tensor", s_h2, pair_free[qq])
                for dc in range(8):
                    k.op("tensor", "matmul", g6, t["wgb"][wp][:, dc, it * 128:(it + 1) * 128],
                         t["xgT"][ep][:, dc, 512:576], start=(dc == 0), stop=(dc == 7))
                for dc in range(8):
                    k.op("tensor", "matmul", u6, t["wub"][wp][:, dc, it * 128:(it + 1) * 128],
                         t["xgT"][ep][:, dc, 512:576],
                         sem=(s_up if dc == 7 else None), start=(dc == 0), stop=(dc == 7))
                k.wait("scalar", s_up, s_up.n)
                vge = k.op("scalar", "activation", t["gsg2"][qq][:, 512:576], g6, sem=s_gelu2, func=AF.Gelu, bias=t["zeroc"][:, :])
                k.wait("vector", s_gelu2, vge)
                pair_free[qq] = k.op("vector", "tensor_tensor", t["hsb"][:, itile, 512:576],
                                     t["gsg2"][qq][:, 512:576], u6, sem=s_h2, op=ALU.mult)
        xgT_free[ep] = s_up.n
        k.wait("tensor", s_h2, s_h2.n)
        for hf in range(2):
            wi = e * 2 + hf
            hp = wi % 2
            k.wait("tensor", s_wd[hp], wd_done[(e, hf)])
            for ci, (c0, cw) in enumerate(CSL):
                rr = (wi * 5 + ci) % 2
                dn = t["PS_B"][:, rr * 512:(rr + 1) * 512]
                if rr in eo_rot_free:
                    k.wait("tensor", s_eov, eo_rot_free[rr])
                for ic in range(32):
                    k.op("tensor", "matmul", dn[0:cw, :],
                         t["hsb"][:, ic, c0:c0 + cw],
                         t["wdb"][hp][:, ic, :],
                         sem=(s_dn if ic == 31 else None), start=(ic == 0), stop=(ic == 31))
                k.wait("vector", s_dn, s_dn.n)
                if rr in eo_store_prev:
                    k.wait("vector", s_eost[rr], eo_store_prev[rr])
                veov = k.op("vector", "tensor_copy", t["eo_sb"][rr][0:cw, :], dn[0:cw, :], sem=s_eov)
                eo_rot_free[rr] = veov
                k.wait("sync", s_eov, veov)
                eo_store_prev[rr] = k.dma("sync", t[f"eo{e}"][c0:c0 + cw, hf * 512:(hf + 1) * 512],
                                          t["eo_sb"][rr][0:cw, :], s_eost[rr])
            wd_free_at[wi] = s_dn.n
        dn_all_prev = s_dn.n
    return (s_eost[0].n, s_eost[1].n)


def emit_bwd(k, app, eost_all, x1st_all, dst):
    t = k.t
    s_eost = [k.sem("eost0"), k.sem("eost1")]
    s_x1st = [k.sem("x1st0"), k.sem("x1st1")]
    s_x3 = k.sem("x3")
    s_bg = [k.sem("bg0"), k.sem("bg1")]
    s_fma = k.sem("fma")
    s_ost = [k.sem("ost0"), k.sem("ost1")]
    k.wait("gpsimd", s_eost[0], eost_all[0])
    k.wait("gpsimd", s_eost[1], eost_all[1])
    k.wait("sync", s_x1st[0], x1st_all[0])
    k.wait("sync", s_x1st[1], x1st_all[1])
    gt_free = {}
    st_at = {}
    for tt in range(NT):
        a0 = t["xt2"][tt % 2]
        a1 = t["xacc"][tt % 2]
        if tt % 2 in st_at:
            k.wait("sync", s_ost[tt % 2], st_at[tt % 2])
        vx3 = k.dma("sync", a0[:, :], t["xmid"][tt * 128:(tt + 1) * 128, :], s_x3)
        k.wait("vector", s_x3, vx3)
        for e in range(NE):
            gp = (tt * NE + e) % 2
            if gp in gt_free:
                k.wait("gpsimd", s_fma, gt_free[gp])
            idx = t["idxb"][:, tt * 16 + e: tt * 16 + e + 1]
            s_bg[gp].n += 16
            def bg(E, o=t["gt"][gp][:, :], ia=t[f"eo{e}"], ix=idx, s=s_bg[gp].h):
                E.indirect_dma_start(out=o, out_offset=None,
                                     in_=ia[:, :], in_offset=IndirectOffsetOnAxis(ap=ix, axis=0)).then_inc(s, 16)
            k.em("gpsimd", bg)
            k.wait("vector", s_bg[gp], s_bg[gp].n)
            src_acc = a0 if e % 2 == 0 else a1
            dst_acc = a1 if e % 2 == 0 else a0
            vfma = k.op("vector", "scalar_tensor_tensor", dst_acc[:, :], t["gt"][gp][:, :],
                 t["weff"][:, tt * 16 + e: tt * 16 + e + 1], src_acc[:, :],
                 sem=s_fma, op0=ALU.mult, op1=ALU.add)
            gt_free[gp] = vfma
        k.wait("sync", s_fma, s_fma.n)
        st_at[tt % 2] = k.dma("sync", dst[tt * 128:(tt + 1) * 128, :], a0[:, :], s_ost[tt % 2])
    return s_ost


def emit_attention(k, gate, ln_base):
    t = k.t
    s_aT = k.sem("aT")
    s_aw = [k.sem("aw0"), k.sem("aw1")]
    s_awo = k.sem("awo")
    s_qk = k.sem("qk")
    s_qke = k.sem("qke")
    s_v = k.sem("v")
    s_ve = k.sem("ve")
    s_sc2 = k.sem("sc2")
    s_ex = k.sem("ex")
    s_av = k.sem("av")
    s_oe = k.sem("oe")
    s_sd = k.sem("sd")
    s_rc = k.sem("rc")
    s_ts = k.sem("ts")
    s_fl = k.sem("fl")
    s_wo = k.sem("wo")
    s_woe = k.sem("woe")
    s_ax = k.sem("ax")
    s_axv = k.sem("axv")
    s_axst = [k.sem(f"axst{i}") for i in range(4)]
    s_xlnst = [k.sem("xlnst0"), k.sem("xlnst1")]

    for gs, gv in gate:
        k.wait("gpsimd", gs, gv)
        k.wait("sync", gs, gv)

    n_qkvw = 0
    qkvw_free_at = {}
    woe_after_batch = None
    ax_st = {}
    for b in range(2):
        k.wait("sync", s_xlnst[0], ln_base[0] + 16 * 8 * (b + 1))
        k.wait("sync", s_xlnst[1], ln_base[1] + 16 * 8 * (b + 1))
        if woe_after_batch is not None:
            k.wait("sync", s_woe, woe_after_batch)
        for dc in range(8):
            k.dma("sync", t["axlnT"][:, dc, :],
                  t["xln"][b * S:(b + 1) * S, dc * 128:(dc + 1) * 128], s_aT, transpose=True)
        vaT = s_aT.n
        k.wait("tensor", s_aT, vaT)
        qk_evac = {}
        for mat in range(2):
            dstq = t["qT"] if mat == 0 else t["kT"]
            for hh in range(8):
                wpar = n_qkvw % 2
                if n_qkvw >= 2:
                    fs, fv = qkvw_free_at[n_qkvw - 2]
                    k.wait("gpsimd", fs, fv)
                k.dma("gpsimd", t["qkvw"][wpar][:, :, :],
                      t["qkvwd"][mat, :, hh * 128:(hh + 1) * 128].rearrange("(dc p) f -> p dc f", p=128), s_aw[wpar])
                cur_qkvw = n_qkvw
                n_qkvw += 1
                k.wait("tensor", s_aw[wpar], s_aw[wpar].n)
                for nch in range(4):
                    rr = nch % 2
                    ps = t["PS_B"][:, rr * 512:(rr + 1) * 512]
                    if rr in qk_evac:
                        k.wait("tensor", s_qke, qk_evac[rr])
                    for dc in range(8):
                        k.op("tensor", "matmul", ps, t["qkvw"][wpar][:, dc, :],
                             t["axlnT"][:, dc, nch * 512:(nch + 1) * 512],
                             sem=(s_qk if dc == 7 else None), start=(dc == 0), stop=(dc == 7))
                    k.wait("vector", s_qk, s_qk.n)
                    qk_evac[rr] = k.op("vector", "tensor_copy",
                                       dstq[:, hh, nch * 512:(nch + 1) * 512], ps, sem=s_qke)
                qkvw_free_at[cur_qkvw] = (s_qk, s_qk.n)
        k.op("vector", "memset",
             t["vsb"][:, :, :].rearrange("p a (h o) -> p a h o", h=16)[:, :, :, 64:65], 1.0)
        vev = None
        for hh in range(8):
            wpar = n_qkvw % 2
            fs, fv = qkvw_free_at[n_qkvw - 2]
            k.wait("gpsimd", fs, fv)
            k.dma("gpsimd", t["qkvw"][wpar][:, :, :],
                  t["qkvwd"][2, :, hh * 128:(hh + 1) * 128].rearrange("(dc p) f -> p dc f", p=128), s_aw[wpar])
            cur_qkvw = n_qkvw
            n_qkvw += 1
            k.wait("tensor", s_aw[wpar], s_aw[wpar].n)
            for ts in range(16):
                if vev is not None:
                    k.wait("tensor", s_ve, vev)
                for dc in range(8):
                    k.op("tensor", "matmul", t["PS_C"][:, 0:128],
                         t["axlnT"][:, dc, ts * 128:(ts + 1) * 128],
                         t["qkvw"][wpar][:, dc, :],
                         sem=(s_v if dc == 7 else None), start=(dc == 0), stop=(dc == 7))
                k.wait("vector", s_v, s_v.n)
                k.op("vector", "tensor_copy", t["vsb"][:, ts, (2 * hh) * 65:(2 * hh) * 65 + 64],
                     t["PS_C"][:, 0:64])
                vev = k.op("vector", "tensor_copy", t["vsb"][:, ts, (2 * hh + 1) * 65:(2 * hh + 1) * 65 + 64],
                           t["PS_C"][:, 64:128], sem=s_ve)
            qkvw_free_at[cur_qkvw] = (s_v, s_v.n)
        oT_evac_prev = None
        for h in range(NH):
            hh, hp = h // 2, h % 2
            kbase = hp * 64
            prev_ex = {}
            prev_av = {}
            for tt in range(16):
                for sc in range(2):
                    i2 = tt * 2 + sc
                    if i2 >= 1 and (i2 - 1) in prev_ex:
                        k.wait("tensor", s_ex, prev_ex[i2 - 1])
                    for j in range(2):
                        k.op("tensor", "matmul", t["PS_B"][:, j * 512:(j + 1) * 512],
                             t["kT"][kbase:kbase + 64, hh, tt * 128:(tt + 1) * 128],
                             t["qT"][kbase:kbase + 64, hh, sc * 1024 + j * 512: sc * 1024 + (j + 1) * 512],
                             sem=(s_sc2 if j == 1 else None), start=True, stop=True)
                    k.wait("scalar", s_sc2, s_sc2.n)
                    par = i2 % 2
                    if par in prev_av:
                        k.wait("scalar", s_av, prev_av[par])
                    vex = k.op("scalar", "activation", t["expT"][par][:, :], t["PS_B"][:, 0:1024],
                               sem=s_ex, func=AF.Exp, scale=ATT_SCALE, bias=t["zeroc"][:, :])
                    prev_ex[i2] = vex
                    k.wait("tensor", s_ex, vex)
                    if tt == 0 and oT_evac_prev is not None:
                        k.wait("tensor", s_oe, oT_evac_prev)
                    for j in range(2):
                        k.op("tensor", "matmul",
                             t["PS_A"][0:65, sc * 1024 + j * 512: sc * 1024 + (j + 1) * 512],
                             t["vsb"][:, tt, h * 65:(h + 1) * 65],
                             t["expT"][par][:, j * 512:(j + 1) * 512],
                             sem=(s_av if j == 1 else None), start=(tt == 0), stop=(tt == 15))
                    prev_av[par] = s_av.n
            k.wait("vector", s_av, s_av.n)
            k.op("vector", "tensor_copy", t["oT_sb"][0:64, :], t["PS_A"][0:64, :])
            oT_evac_prev = k.op("vector", "tensor_copy", t["sums"][0:1, :], t["PS_A"][64:65, :], sem=s_oe)
            k.wait("tensor", s_oe, oT_evac_prev)
            for ts in range(16):
                k.op("tensor", "matmul", t["PS_C"][:, ts:ts + 1],
                     t["sums"][0:1, ts * 128:(ts + 1) * 128], t["ones_row"][0:1, 0:1],
                     sem=(s_sd if ts == 15 else None), start=True, stop=True)
            k.wait("vector", s_sd, s_sd.n)
            vrc = k.op("vector", "reciprocal", t["rcol"][:, :], t["PS_C"][:, 0:16], sem=s_rc)
            k.wait("tensor", s_rc, vrc)
            for ts in range(16):
                v1 = k.op("tensor", "transpose", t["PT"][:, 0:64],
                          t["oT_sb"][0:64, ts * 128:(ts + 1) * 128], t["c_ident_bf"][0:64, 0:64], sem=s_ts)
                k.wait("vector", s_ts, v1)
                v2 = k.op("vector", "tensor_scalar_mul", t["otk"][:, :], t["PT"][:, 0:64],
                          t["rcol"][:, ts:ts + 1], sem=s_ts)
                k.wait("tensor", s_ts, v2)
                v3 = k.op("tensor", "transpose", t["PT"][0:64, 64:192], t["otk"][:, :],
                          t["c_ident_bf"][:, :], sem=s_ts)
                k.wait("vector", s_ts, v3)
                v4 = k.op("vector", "tensor_copy",
                          t["flatT"][kbase:kbase + 64, hh, ts * 128:(ts + 1) * 128],
                          t["PT"][0:64, 64:192], sem=s_fl)
                k.wait("tensor", s_fl, v4)
        k.wait("gpsimd", s_fl, s_fl.n)
        if woe_after_batch is not None:
            k.wait("gpsimd", s_woe, woe_after_batch)
        for fq in range(4):
            vwo = k.dma("gpsimd", t["wo_sb"][:, fq * 2:(fq + 1) * 2, :],
                        t["wod"][fq * 256:(fq + 1) * 256, :].rearrange("(fc p) m -> p fc m", p=128), s_awo)
        k.wait("tensor", s_awo, vwo)
        for scq in range(4):
            for j4 in range(4):
                stile = scq * 4 + j4
                if j4 in ax_st:
                    k.wait("sync", s_axst[j4], ax_st[j4])
                k.dma("sync", t["axc"][j4][:, :],
                      t["xmid"][(b * 16 + stile) * 128:(b * 16 + stile + 1) * 128, :], s_ax)
            vax = s_ax.n
            k.wait("vector", s_ax, vax)
            woe_prev = {}
            for mt in range(8):
                rr = mt % 2
                ps = t["PS_B"][:, rr * 512:(rr + 1) * 512]
                if rr in woe_prev:
                    k.wait("tensor", s_woe, woe_prev[rr])
                for fc in range(8):
                    k.op("tensor", "matmul", ps, t["wo_sb"][:, fc, mt * 128:(mt + 1) * 128],
                         t["flatT"][:, fc, scq * 512:(scq + 1) * 512],
                         sem=(s_wo if fc == 7 else None), start=(fc == 0), stop=(fc == 7))
                k.wait("vector", s_wo, s_wo.n)
                vwoe = k.op("vector", "tensor_copy", t["wotmp"][rr][:, :], ps, sem=s_woe)
                woe_prev[rr] = vwoe
                k.wait("tensor", s_woe, vwoe)
                for j4 in range(4):
                    pslot = 192 + (j4 % 2) * 128
                    vt = k.op("tensor", "transpose", t["PT"][:, pslot:pslot + 128],
                              t["wotmp"][rr][:, j4 * 128:(j4 + 1) * 128], t["c_ident_bf"][:, :], sem=s_ts)
                    k.wait("vector", s_ts, vt)
                    vv = k.op("vector", "tensor_add", t["axc"][j4][:, mt * 128:(mt + 1) * 128],
                              t["axc"][j4][:, mt * 128:(mt + 1) * 128],
                              t["PT"][:, pslot:pslot + 128], sem=s_axv)
                    k.wait("tensor", s_axv, vv)
            k.wait("sync", s_axv, s_axv.n)
            for j4 in range(4):
                stile = scq * 4 + j4
                ax_st[j4] = k.dma("sync", t["xmid"][(b * 16 + stile) * 128:(b * 16 + stile + 1) * 128, :],
                                  t["axc"][j4][:, :], s_axst[j4])
        woe_after_batch = s_woe.n
    return s_axst


def build(stage="full"):
    nc = bass.Bass("TRN2", target_bir_lowering=False)
    ctx = ExitStack()
    k = K(nc, ctx)
    t = k.t
    sa = SA(nc)

    def din(name, shape, dtype=F32):
        t[name] = nc.dram_tensor(name, list(shape), dtype, kind="ExternalInput").ap()
    din("x_in", [TLOC, D])
    din("qkvwd", [3, D, D])
    din("wod", [D, D])
    din("gatewd", [D, 16])
    din("wgt", [NE, D, 4096])
    din("wut", [NE, D, 4096])
    din("wdt", [NE, 4096, D])
    din("sgt", [D, 2048])
    din("sut", [D, 2048])
    din("sdt", [2048, D])
    din("c_t1d", [128, 128], BF16)
    din("c_tfulld", [128, 128], BF16)
    din("c_tdiagd", [128, 128], BF16)
    din("c_identbd", [128, 128], BF16)
    din("c_identfd", [128, 128], F32)
    din("c_ones128d", [128, 128], BF16)
    din("c_onescold", [128, 1], BF16)
    din("c_bigiotad", [128, 1], F32)
    t["y"] = nc.dram_tensor("y", [TLOC, D], F32, kind="ExternalOutput").ap()
    t["xmid"] = nc.dram_tensor("xmid", [TLOC, D], F32).ap()
    t["xln"] = nc.dram_tensor("xln", [TLOC, D], BF16).ap()
    t["sc_loc"] = nc.dram_tensor("sc_loc", [TLOC, 16], F32).ap()
    t["sc_glob"] = nc.dram_tensor("sc_glob", [TLOC * NCORES, 16], F32).ap()
    t["rowtmp"] = nc.dram_tensor("rowtmp", [512, 1], F32).ap()
    for e in range(NE):
        t[f"xg{e}"] = nc.dram_tensor(f"xg{e}", [CPAD, D], BF16).ap()
        t[f"eo{e}"] = nc.dram_tensor(f"eo{e}", [CPAD, D], BF16).ap()

    t["xt"] = [sa.pers(f"xt{i}", [128, 1024], F32) for i in range(2)]
    t["xlt"] = [sa.pers(f"xlt{i}", [128, 1024], BF16) for i in range(2)]
    t["xt2"] = [sa.pers(f"xt2_{i}", [128, 1024], F32) for i in range(2)]
    t["xacc"] = [sa.pers(f"xacc{i}", [128, 1024], F32) for i in range(2)]
    t["bst"] = sa.pers("bst", [128, 12], F32)
    t["mv"] = sa.pers("mv", [128, 2], F32)
    t["stdv"] = sa.pers("stdv", [128, 1], F32)
    t["rstd"] = sa.pers("rstd", [128, 1], F32)
    t["nmr"] = sa.pers("nmr", [128, 1], F32)
    t["gw"] = sa.pers("gw", [128, 8, 16], BF16)
    t["gmax"] = sa.pers("gmax", [128, 1], F32)
    t["gneg"] = sa.pers("gneg", [128, 1], F32)
    t["gexp"] = sa.pers("gexp", [128, 16], F32)
    t["gsum"] = sa.pers("gsum", [128, 1], F32)
    t["grec"] = sa.pers("grec", [128, 1], F32)
    t["scores"] = sa.pers("scores", [128, 512], F32)
    t["theta"] = sa.pers("theta", [128, 16], F32)
    t["red"] = sa.pers("red", [128, 16], BF16)
    t["dir"] = sa.pers("dir", [128, 16], F32)
    t["cs_bf"] = sa.pers("cs_bf", [128, 4], BF16)
    t["off_f"] = sa.pers("off_f", [128, 4], F32)
    t["orow"] = sa.pers("orow", [1, 512], F32)
    t["ones_row"] = sa.pers("ones_row", [1, 128], F32)
    t["mask"] = sa.pers("mask", [128, 512], BF16)
    t["rank"] = sa.pers("rank", [128, 512], F32)
    t["idxf"] = sa.pers("idxf", [128, 512], U32)
    t["idxb"] = sa.pers("idxb", [128, 512], U32)
    t["weff"] = sa.pers("weff", [128, 512], F32)
    t["gsg"] = [sa.pers(f"gsg{i}", [128, 512], BF16) for i in range(2)]
    t["gsg2"] = [sa.pers(f"gsg2_{i}", [128, 576], BF16) for i in range(2)]
    t["eo_sb"] = [sa.pers(f"eo_sb{i}", [128, 512], BF16) for i in range(2)]
    t["c_t1"] = sa.pers("c_t1", [128, 128], BF16)
    t["c_tfull"] = sa.pers("c_tfull", [128, 128], BF16)
    t["c_tdiag"] = sa.pers("c_tdiag", [128, 128], BF16)
    t["c_ident_bf"] = sa.pers("c_ident_bf", [128, 128], BF16)
    t["c_ident_f32"] = sa.pers("c_ident_f32", [128, 128], F32)
    t["c_ones128"] = sa.pers("c_ones128", [128, 128], BF16)
    t["c_ones_col"] = sa.pers("c_ones_col", [128, 1], BF16)
    t["c_bigiota"] = sa.pers("c_bigiota", [128, 1], F32)
    t["epsc"] = sa.pers("epsc", [128, 1], F32)
    t["zeroc"] = sa.pers("zeroc", [128, 1], F32)
    sa.close()

    u = 0
    t["sgw"] = sa.union("sgw", [128, 8, 2048], BF16, u); u += 32768
    t["suw"] = sa.union("suw", [128, 8, 2048], BF16, u); u += 32768
    t["sdw"] = sa.union("sdw", [128, 16, 1024], BF16, u); u += 32768
    t["hsh"] = sa.union("hsh", [128, 16, 512], BF16, u); u += 16384
    t["xlnT"] = []
    for i in range(2):
        t["xlnT"].append(sa.union(f"xlnT{i}", [128, 8, 512], BF16, u)); u += 8192
    u = 0
    t["scg"] = sa.union("scg", [128, 256, 16], F32, 0)
    t["cmp"] = sa.union("cmp", [128, 4096], BF16, 16384)
    t["prefw"] = sa.union("prefw", [128, 512], F32, 32768)
    t["notm"] = sa.union("notm", [128, 512], F32, 34816)
    t["idxf_f"] = sa.union("idxf_f", [128, 512], F32, 36864)
    t["idxb_f"] = sa.union("idxb_f", [128, 512], F32, 38912)
    t["rok"] = sa.union("rok", [128, 512], F32, 40960)
    t["gt"] = [sa.union(f"gt{i}", [128, 1024], BF16, 43008 + i * 2048) for i in range(2)]
    t["wgb"] = []
    t["wub"] = []
    for i in range(2):
        t["wgb"].append(sa.union(f"wgb{i}", [128, 8, 512], BF16, u)); u += 8192
    for i in range(2):
        t["wub"].append(sa.union(f"wub{i}", [128, 8, 512], BF16, u)); u += 8192
    t["wdb"] = []
    for i in range(2):
        t["wdb"].append(sa.union(f"wdb{i}", [128, 32, 512], BF16, u)); u += 32768
    t["hsb"] = sa.union("hsb", [128, 32, 576], BF16, u); u += 36864
    t["xgT"] = []
    for i in range(2):
        t["xgT"].append(sa.union(f"xgT{i}", [128, 8, 576], BF16, u)); u += 9216
    u = 0
    t["axlnT"] = sa.union("axlnT", [128, 8, 2048], BF16, u)
    t["flatT"] = sa.union("flatT", [128, 8, 2048], BF16, u); u += 32768
    t["qT"] = sa.union("qT", [128, 8, 2048], BF16, u); u += 32768
    u_kT = u
    t["kT"] = sa.union("kT", [128, 8, 2048], BF16, u); u += 32768
    t["vsb"] = sa.union("vsb", [128, 16, 1040], BF16, u); u += 33280
    t["qkvw"] = []
    for i in range(2):
        t["qkvw"].append(sa.union(f"qkvw{i}", [128, 8, 128], BF16, u)); u += 2048
    t["expT"] = []
    for i in range(2):
        t["expT"].append(sa.union(f"expT{i}", [128, 1024], BF16, u)); u += 2048
    t["oT_sb"] = sa.union("oT_sb", [128, 2048], BF16, u); u += 4096
    t["sums"] = sa.union("sums", [1, 2048], F32, u); u += 8192
    t["rcol"] = sa.union("rcol", [128, 16], F32, u); u += 64
    t["otk"] = sa.union("otk", [128, 64], BF16, u); u += 128
    t["wotmp"] = []
    for i in range(2):
        t["wotmp"].append(sa.union(f"wotmp{i}", [128, 512], BF16, u)); u += 1024
    t["axc"] = [t["xt2"][0], t["xt2"][1], t["xacc"][0], t["xacc"][1]]
    t["wo_sb"] = sa.union("wo_sb", [128, 8, 1024], BF16, u_kT)

    t["PS_A"] = ctx.enter_context(nc.psum_tensor("PS_A", [128, 2048], F32)).ap()
    t["PS_B"] = ctx.enter_context(nc.psum_tensor("PS_B", [128, 1024], F32)).ap()
    t["PS_C"] = ctx.enter_context(nc.psum_tensor("PS_C", [128, 512], F32)).ap()
    t["PT"] = ctx.enter_context(nc.psum_tensor("PT", [128, 448], BF16)).ap()

    def _bc_init(E, _k=k):
        _k.bc_reg = E.alloc_register("bcreg")
        E.reg_mov(_k.bc_reg, CPAD - 1)
    k.em("gpsimd", _bc_init)
    s_init = k.sem("init")
    s_initg = k.sem("initg")
    s_zmem = k.sem("zmem")
    s_zero = k.sem("zero")
    k.dma("gpsimd", t["gw"][:, :, :], t["gatewd"].rearrange("(dc p) e -> p dc e", p=128), s_initg)
    for nm, src in [("c_t1", "c_t1d"), ("c_tfull", "c_tfulld"), ("c_tdiag", "c_tdiagd"),
                    ("c_ident_bf", "c_identbd"), ("c_ident_f32", "c_identfd"),
                    ("c_ones128", "c_ones128d"), ("c_ones_col", "c_onescold"),
                    ("c_bigiota", "c_bigiotad")]:
        k.dma("scalar", t[nm][:, :], t[src], s_init)
    k.op("vector", "memset", t["ones_row"][0:1, :], 1.0)
    k.op("vector", "memset", t["epsc"][:, :], 1e-5)
    k.op("vector", "memset", t["zeroc"][:, :], 0.0)
    vz = k.op("vector", "memset", t["gt"][0][:, :], 0.0, sem=s_zmem)
    k.wait("scalar", s_zmem, vz)
    for e in range(NE):
        for r0, rw in CSL:
            k.dma("scalar", t[f"xg{e}"][r0:r0 + rw, :], t["gt"][0][0:rw, :], s_zero)
    init_all = s_init.n
    for e in ENGINES:
        k.wait(e, s_init, init_all)
        k.wait(e, s_initg, s_initg.n)
    k.wait("vector", s_zero, s_zero.n)
    k.wait("gpsimd", s_zero, s_zero.n)

    def ffn(app, src, dst, gate):
        _, lnb = emit_ln(k, src, f"{app}f", pre=gate and ("sync", gate[0], gate[1]))
        x1_all = emit_gate_shared(k, app, src, t["xmid"], gate, lnb)
        rk_val = emit_routing(k, app)
        group_done = emit_fwd_scatter(k, app, rk_val)
        shared_done = k.sem("x1v").n
        eost_all = emit_experts(k, app, group_done, rk_val, shared_done)
        return emit_bwd(k, app, eost_all, x1_all, dst)

    def ost_pairs(s_o):
        return [(s_o[0], s_o[0].n), (s_o[1], s_o[1].n)]

    if stage.startswith("dbg_"):
        which = stage[4:]
        s_d = k.sem("dbg")
        if which == "ln":
            sem, lnb = emit_ln(k, t["x_in"], "0f")
            k.wait("gpsimd", sem[0], sem[0].n)
            k.wait("gpsimd", sem[1], sem[1].n)
            k.dma("gpsimd", t["y"][:, :], t["xln"][:, :], s_d)
        elif which in ("shared", "shared2", "routing", "scatter", "experts"):
            _, lnb = emit_ln(k, t["x_in"], "0f")
            x1_all = emit_gate_shared(k, 0, t["x_in"], t["xmid"], None, lnb)
            if which == "shared":
                k.wait("gpsimd", k.sem("x1st0"), x1_all[0])
                k.wait("gpsimd", k.sem("x1st1"), x1_all[1])
                k.dma("gpsimd", t["y"][:, :], t["xmid"][:, :], s_d)
            elif which == "shared2":
                k.wait("gpsimd", k.sem("x1st0"), x1_all[0])
                k.wait("gpsimd", k.sem("x1st1"), x1_all[1])
                k.wait("gpsimd", k.sem("hsh"), k.sem("hsh").n)
                k.dma("gpsimd", t["y"][0:128, :], t["hsh"][:, 0:2, :], s_d)
                k.dma("gpsimd", t["y"][128:256, :], t["xlnT"][1][:, 0:2, :], s_d)
                k.dma("gpsimd", t["y"][256:384, :], t["sgw"][:, 0:1, 0:1024], s_d)
                k.dma("gpsimd", t["y"][384:512, :], t["sdw"][:, 0:1, :], s_d)
                k.dma("gpsimd", t["y"][512:640, 0:512], t["gsg"][1][:, :], s_d)
                k.dma("gpsimd", t["y"][640:768, 0:512], t["gsg"][0][:, :], s_d)
                k.dma("gpsimd", t["y"][768:896, 0:512], t["hsh"][:, 15, :], s_d)
                k.dma("gpsimd", t["y"][896:1024, 0:512], t["hsh"][:, 14, :], s_d)
            else:
                rk_val = emit_routing(k, 0)
                if which == "routing":
                    k.wait("gpsimd", k.sem("rk"), rk_val)
                    k.wait("gpsimd", k.sem("x1st0"), x1_all[0])
                    k.wait("gpsimd", k.sem("x1st1"), x1_all[1])
                    k.dma("gpsimd", t["y"][0:128, 0:512].rearrange("p f -> p f"), t["idxf_f"][:, :], s_d)
                    k.dma("gpsimd", t["y"][128:256, 0:512], t["weff"][:, :], s_d)
                    k.dma("gpsimd", t["y"][256:384, 0:512], t["scores"][:, :], s_d)
                    k.dma("gpsimd", t["y"][384:512, 0:16], t["theta"][:, :], s_d)
                    k.dma("gpsimd", t["y"][512:640, 0:512], t["rank"][:, :], s_d)
                else:
                    group_done = emit_fwd_scatter(k, 0, rk_val)
                    if which == "scatter":
                        for grp in group_done:
                            for gsem, gval in grp:
                                k.wait("gpsimd", gsem, gval)
                        k.wait("gpsimd", k.sem("x1st0"), x1_all[0])
                        k.wait("gpsimd", k.sem("x1st1"), x1_all[1])
                        for e in range(7):
                            k.dma("gpsimd", t["y"][e * 576:(e + 1) * 576, :], t[f"xg{e}"][:, :], s_d)
                    else:
                        shared_done = k.sem("x1v").n
                        eost_all = emit_experts(k, 0, group_done, rk_val, shared_done)
                        k.wait("gpsimd", k.sem("eost0"), eost_all[0])
                        k.wait("gpsimd", k.sem("eost1"), eost_all[1])
                        k.wait("gpsimd", k.sem("x1st0"), x1_all[0])
                        k.wait("gpsimd", k.sem("x1st1"), x1_all[1])
                        for e in range(7):
                            k.dma("gpsimd", t["y"][e * 576:(e + 1) * 576, :], t[f"eo{e}"][:, :], s_d)
        k.wait("gpsimd", s_d, s_d.n)
        k.wait("sync", s_d, s_d.n)
    elif stage == "ffn0":
        s_o = ffn(0, t["x_in"], t["y"], None)
        for sem, val in ost_pairs(s_o):
            k.wait("gpsimd", sem, val)
            k.wait("sync", sem, val)
    else:
        s_o0 = ffn(0, t["x_in"], t["xmid"], None)
        o0p = ost_pairs(s_o0)
        _, lnbA = emit_ln(k, t["xmid"], "A", pre=o0p)
        s_att = emit_attention(k, o0p, lnbA)
        attp = [(s, s.n) for s in s_att]
        if stage == "attn":
            s_cp = k.sem("cpout")
            for sem, val in attp:
                k.wait("sync", sem, val)
            for tt in range(NT):
                k.dma("sync", t["y"][tt * 128:(tt + 1) * 128, :], t["xmid"][tt * 128:(tt + 1) * 128, :], s_cp)
            k.wait("gpsimd", s_cp, s_cp.n)
            k.wait("sync", s_cp, s_cp.n)
        else:
            _, lnb1 = emit_ln(k, t["xmid"], "1f", pre=attp)
            x1_all = emit_gate_shared(k, 1, t["xmid"], t["xmid"], attp[0], lnb1)
            for sem, val in attp[1:]:
                k.wait("gpsimd", sem, val)
            rk_val = emit_routing(k, 1)
            group_done = emit_fwd_scatter(k, 1, rk_val)
            shared_done = k.sem("x1v").n
            eost_all = emit_experts(k, 1, group_done, rk_val, shared_done)
            s_o = emit_bwd(k, 1, eost_all, x1_all, t["y"])
            for sem, val in ost_pairs(s_o):
                k.wait("gpsimd", sem, val)
                k.wait("sync", sem, val)

    with nc.Block() as block:
        @block.sync
        def _(E):
            for f in k.streams["sync"]:
                f(E)
        @block.scalar
        def _(E):
            for f in k.streams["scalar"]:
                f(E)
        @block.tensor
        def _(E):
            for f in k.streams["tensor"]:
                f(E)
        @block.vector
        def _(E):
            for f in k.streams["vector"]:
                f(E)
        @block.gpsimd
        def _(E):
            for f in k.streams["gpsimd"]:
                f(E)
    actual = {}
    for bb in nc.main_func.blocks:
        for ins in bb.instructions:
            si = getattr(ins, "sync_info", None)
            if si is not None and si.on_update:
                for u in si.on_update:
                    m = str(u)
                    import re as _re
                    mm_ = _re.search(r"id=(\d+).*?update_value=(\d+)", m)
                    if mm_:
                        sid, val = int(mm_.group(1)), int(mm_.group(2))
                        actual[sid] = actual.get(sid, 0) + val
    for name, sem in k.sems.items():
        sid = sem.h.num
        if sem.n != actual.get(sid, 0):
            raise AssertionError(f"sem {name} (id {sid}): counter {sem.n} != actual {actual.get(sid, 0)}")
    return nc, k


def host_inputs(inputs):
    bf = ml_dtypes.bfloat16
    kern = np.ascontiguousarray(np.asarray(inputs["L0_attn_kernel"]).transpose(0, 2, 1, 3).reshape(3, 1024, 1024))
    wo = np.ascontiguousarray(np.asarray(inputs["L0_attn_WO"])[0].reshape(64, 16, 1024).transpose(1, 0, 2).reshape(1024, 1024))
    gw = np.ascontiguousarray(np.asarray(inputs["L0_ffn_gate_w"]).T)
    wgt = np.ascontiguousarray(np.asarray(inputs["L0_ffn_Wg"]).transpose(0, 2, 1))
    wut = np.ascontiguousarray(np.asarray(inputs["L0_ffn_Wu"]).transpose(0, 2, 1))
    wdt = np.ascontiguousarray(np.asarray(inputs["L0_ffn_Wd"]).transpose(0, 2, 1))
    sgt = np.ascontiguousarray(np.asarray(inputs["L0_ffn_Sg"]).T)
    sut = np.ascontiguousarray(np.asarray(inputs["L0_ffn_Su"]).T)
    sdt = np.ascontiguousarray(np.asarray(inputs["L0_ffn_Sd"]).T)
    i128 = np.arange(128)
    c_t1 = (i128[:, None] < i128[None, :]).astype(bf)
    ke = i128 % 16
    kt = i128 // 16
    c_tfull = (ke[:, None] == ke[None, :]).astype(bf)
    c_tdiag = ((ke[:, None] == ke[None, :]) & (kt[:, None] < kt[None, :])).astype(bf)
    common = dict(
        qkvwd=kern, wod=wo, gatewd=gw, wgt=wgt, wut=wut, wdt=wdt,
        sgt=sgt, sut=sut, sdt=sdt,
        c_t1d=c_t1, c_tfulld=c_tfull, c_tdiagd=c_tdiag,
        c_identbd=np.eye(128, dtype=bf), c_identfd=np.eye(128, dtype=np.float32),
        c_ones128d=np.ones((128, 128), dtype=bf), c_onescold=np.ones((128, 1), dtype=bf),
        c_bigiotad=(BIGIOTA + i128[:, None]).astype(np.float32))
    x = np.asarray(inputs["L0_attn_x"], dtype=np.float32).reshape(8, TLOC, D)
    return [dict(common, x_in=np.ascontiguousarray(x[c])) for c in range(NCORES)]


def kernel(**inputs):
    maps = host_inputs(inputs)
    nc, _ = build("full")
    res = run_bass_kernel_spmd(nc, maps, core_ids=list(range(NCORES)))
    out = np.concatenate([r["y"].reshape(2, 2048, 1024) for r in res.results], axis=0)
    return out.astype(np.float32)


# version 02 (final)
# speedup vs baseline: 1.0386x; 1.0386x over previous
import os
from contextlib import ExitStack

import numpy as np
import ml_dtypes

import concourse.bass as bass
import concourse.mybir as mybir
from concourse.bass_utils import run_bass_kernel_spmd
from concourse.bass_core_rust import IndirectOffsetOnAxis

F32 = mybir.dt.float32
BF16 = mybir.dt.bfloat16
U32 = mybir.dt.uint32
AF = mybir.ActivationFunctionType
ALU = mybir.AluOpType
AX = mybir.AxisListType
AP = bass.AP

NCORES = 8
D = 1024
TLOC = 4096
NT = 32
NE = 16
CPAD = 576
CSL = [(0, 128), (128, 128), (256, 128), (384, 128), (512, 64)]
NBIS = 26
SLAB = 512
NSLAB = TLOC // SLAB
S = 2048
NH = 16
ATT_SCALE = 1.0 / (64 * 64 * 16)
BIGIOTA = 65536.0

ENGINES = ("sync", "scalar", "tensor", "vector", "gpsimd")


class Sem:
    __slots__ = ("h", "n", "name")

    def __init__(self, ctx, nc, name):
        self.h = ctx.enter_context(nc.semaphore(name))
        self.n = 0
        self.name = name


def _ap_tname(a):
    try:
        return a.tensor.name
    except Exception:
        return None


class K:
    def __init__(self, nc, ctx):
        self.nc = nc
        self.ctx = ctx
        self.streams = {e: [] for e in ENGINES}
        self.sems = {}
        self.t = {}
        self.bar_k = 0
        self.dirty = {e: set() for e in ENGINES}
        self.n_drain = 0

    def sem(self, name):
        if name not in self.sems:
            self.sems[name] = Sem(self.ctx, self.nc, name)
        return self.sems[name]

    def em(self, eng, fn):
        self.streams[eng].append(fn)

    def dma(self, eng, out, in_, sem, **kw):
        sem.n += 16
        self.em(eng, lambda E, o=out, i=in_, s=sem.h, kw=dict(kw): E.dma_start(out=o, in_=i, **kw).then_inc(s, 16))
        return sem.n

    def wait(self, eng, sem, val):
        self.em(eng, lambda E, s=sem.h, v=val: E.wait_ge(s, v))

    def drain(self, eng):
        self.em(eng, lambda E: E.drain())
        self.dirty[eng].clear()
        self.n_drain += 1

    def op(self, eng, meth, *args, sem=None, extra_incs=(), allow_lp=False, **kw):
        if eng in ("vector", "scalar", "gpsimd"):
            ins = [a for a in args[1:] if isinstance(a, AP)]
            for key in ("bias", "scale"):
                if isinstance(kw.get(key), AP):
                    ins.append(kw[key])
            innames = {_ap_tname(a) for a in ins}
            if innames & self.dirty[eng]:
                self.drain(eng)
            out_name = _ap_tname(args[0])
            if out_name:
                self.dirty[eng].add(out_name)
        v = None
        if sem is not None:
            sem.n += 1
            v = sem.n
        for s2 in extra_incs:
            s2.n += 1
        def fn(E, m=meth, a=args, kw=dict(kw), s=sem.h if sem else None,
               xs=tuple(s2.h for s2 in extra_incs), lp=allow_lp):
            if lp:
                with E.bass.allow_low_precision(reason="small exact integers"):
                    ins = getattr(E, m)(*a, **kw)
            else:
                ins = getattr(E, m)(*a, **kw)
            if s is not None:
                ins.then_inc(s, 1)
            for s2 in xs:
                ins.then_inc(s2, 1)
        self.em(eng, fn)
        return v

    def sem_inc(self, eng, sem):
        sem.n += 1
        self.em(eng, lambda E, s=sem.h: E.sem_inc(s, 1))
        return sem.n


class SA:
    def __init__(self, nc):
        self.nc = nc
        self.limit = nc.SBUF_PARTITION_SIZE_BYTES
        base = 0
        if nc.dma_scratch is not None:
            ml = nc.lookup_mloc(nc.dma_scratch)
            base = (ml.addr + 16384 + 127) // 128 * 128
        self.off = base
        self.union_base = None
        self.peak = 0

    @staticmethod
    def _per(shape, dtype):
        per = int(np.prod(shape[1:])) * mybir.dt.size(dtype)
        return (per + 31) // 32 * 32

    def pers(self, name, shape, dtype):
        ap = self.nc.alloc_sbuf_tensor_at(name, list(shape), dtype, offset=self.off).ap()
        self.off += self._per(shape, dtype)
        assert self.off <= self.limit, f"SBUF pers overflow at {name}: {self.off}"
        return ap

    def close(self):
        self.union_base = self.off

    def union(self, name, shape, dtype, uoff):
        off = self.union_base + uoff
        end = off + self._per(shape, dtype)
        self.peak = max(self.peak, end)
        assert end <= self.limit, f"SBUF union overflow at {name}: {end} > {self.limit}"
        return self.nc.alloc_sbuf_tensor_at(name, list(shape), dtype, offset=off).ap()


def emit_ln(k, src, tag, pre=None):
    t = k.t
    s_xld = k.sem("xld")
    s_mv = k.sem("mv")
    s_std = k.sem("std")
    s_xlnv = k.sem("xlnv")
    s_xlnst = [k.sem("xlnst0"), k.sem("xlnst1")]
    ln_base = (s_xlnst[0].n, s_xlnst[1].n)
    if pre is not None:
        for sem, val in pre:
            k.wait("sync", sem, val)
    free_at = {}
    store_at = {}
    for tt in range(NT):
        par = tt % 2
        xt = t["xt"][par]
        xlt = t["xlt"][par]
        if par in free_at:
            k.wait("sync", s_xlnv, free_at[par])
        vld = k.dma("sync", xt[:, :], src[tt * 128:(tt + 1) * 128, :], s_xld)
        k.wait("vector", s_xld, vld)
        k.op("vector", "bn_stats", t["bst"][:, 0:6], xt[:, 0:512])
        k.op("vector", "bn_stats", t["bst"][:, 6:12], xt[:, 512:1024])
        vmv = k.op("vector", "bn_aggr", t["mv"][:, :], t["bst"][:, 0:12], sem=s_mv)
        k.wait("scalar", s_mv, vmv)
        vstd = k.op("scalar", "activation", t["stdv"][:, :], t["mv"][:, 1:2], sem=s_std, func=AF.Sqrt, bias=t["epsc"][:, :])
        k.wait("vector", s_std, vstd)
        k.op("vector", "reciprocal", t["rstd"][:, :], t["stdv"][:, :])
        k.op("vector", "tensor_mul", t["nmr"][:, :], t["mv"][:, 0:1], t["rstd"][:, :])
        if par in store_at:
            k.wait("vector", s_xlnst[par], store_at[par])
        vxln = k.op("vector", "tensor_scalar", xlt[:, :], xt[:, :], t["rstd"][:, :], t["nmr"][:, :],
                    sem=s_xlnv, op0=ALU.mult, op1=ALU.subtract)
        free_at[par] = vxln
        k.wait("sync", s_xlnv, vxln)
        store_at[par] = k.dma("sync", t["xln"][tt * 128:(tt + 1) * 128, :], xlt[:, :], s_xlnst[par])
    return s_xlnst, ln_base


def emit_gate_shared(k, app, src, xmid, gate, ln_base):
    t = k.t
    s_xlnst = [k.sem("xlnst0"), k.sem("xlnst1")]
    s_xT = k.sem("xT")
    s_gmm = k.sem("gmm")
    s_gact = k.sem("gact")
    s_gsm = k.sem("gsm")
    s_scst = k.sem("scst")
    s_upmm = k.sem("upmm")
    s_gelu = k.sem("gelu")
    s_hsh = k.sem("hsh")
    s_dnmm = k.sem("dnmm")
    s_x2 = k.sem("x2")
    s_x1v = k.sem("x1v")
    s_x1st = [k.sem("x1st0"), k.sem("x1st1")]
    s_wld = k.sem("wldB")

    if gate is not None:
        gs, gv = gate
        k.wait("gpsimd", gs, gv)
    s_wldd = k.sem("wldD")
    k.dma("gpsimd", t["sgw"][:, :, :], t["sgt"].rearrange("(dc p) i -> p dc i", p=128), s_wld)
    k.dma("gpsimd", t["suw"][:, :, :], t["sut"].rearrange("(dc p) i -> p dc i", p=128), s_wld)
    wld_up = s_wld.n
    k.dma("gpsimd", t["sdw"][:, :, :], t["sdt"].rearrange("(ic p) d -> p ic d", p=128), s_wldd)
    wld_dn = s_wldd.n
    k.wait("tensor", s_wld, wld_up)

    xT_free = {}
    x1_st = {}
    pair_free = {}
    dn_free = {}
    gate_ps_free = None
    for sl in range(NSLAB):
        spar = sl % 2
        xlnT = t["xlnT"][spar]
        if spar in xT_free:
            k.wait("sync", s_upmm, xT_free[spar])
        k.wait("sync", s_xlnst[0], ln_base[0] + 16 * 2 * (sl + 1))
        k.wait("sync", s_xlnst[1], ln_base[1] + 16 * 2 * (sl + 1))
        for dc in range(8):
            k.dma("sync", xlnT[:, dc, :],
                  t["xln"][sl * SLAB:(sl + 1) * SLAB, dc * 128:(dc + 1) * 128], s_xT, transpose=True)
        vxT = s_xT.n
        k.wait("tensor", s_xT, vxT)
        for ts in ([] if os.environ.get("KSKIP_GATE") else range(4)):
            tile = sl * 4 + ts
            if gate_ps_free is not None:
                k.wait("tensor", s_gact, gate_ps_free)
            for dc in range(8):
                k.op("tensor", "matmul", t["PS_C"][:, 0:16],
                     xlnT[:, dc, ts * 128:(ts + 1) * 128], t["gw"][:, dc, :],
                     sem=(s_gmm if dc == 7 else None), start=(dc == 0), stop=(dc == 7))
            k.wait("vector", s_gmm, s_gmm.n)
            k.op("vector", "tensor_reduce", t["gmax"][:, :], t["PS_C"][:, 0:16], AX.X, ALU.max)
            vneg = k.op("vector", "tensor_scalar_mul", t["gneg"][:, :], t["gmax"][:, :], -1.0, sem=s_gsm)
            k.wait("scalar", s_gsm, vneg)
            gate_ps_free = k.op("scalar", "activation", t["gexp"][:, :], t["PS_C"][:, 0:16],
                                sem=s_gact, func=AF.Exp, bias=t["gneg"][:, :])
            k.wait("vector", s_gact, gate_ps_free)
            k.op("vector", "reduce_sum", t["gsum"][:, :], t["gexp"][:, :], AX.X)
            k.op("vector", "reciprocal", t["grec"][:, :], t["gsum"][:, :])
            vsc = k.op("vector", "tensor_scalar_mul",
                       t["scores"][:, tile * 16:(tile + 1) * 16], t["gexp"][:, :], t["grec"][:, :], sem=s_gsm)
            k.wait("sync", s_gsm, vsc)
            k.dma("sync", t["sc_loc"][tile * 128:(tile + 1) * 128, :],
                  t["scores"][:, tile * 16:(tile + 1) * 16], s_scst)
        for ist in range(16):
            gi = sl * 16 + ist
            qq = gi % 2
            sgp = t["PS_A"][:, qq * 1024: qq * 1024 + 512]
            sup = t["PS_A"][:, qq * 1024 + 512: qq * 1024 + 1024]
            if qq in pair_free:
                k.wait("tensor", s_hsh, pair_free[qq])
            for dc in range(8):
                k.op("tensor", "matmul", sgp, t["sgw"][:, dc, ist * 128:(ist + 1) * 128],
                     xlnT[:, dc, :], start=(dc == 0), stop=(dc == 7))
            for dc in range(8):
                k.op("tensor", "matmul", sup, t["suw"][:, dc, ist * 128:(ist + 1) * 128],
                     xlnT[:, dc, :],
                     sem=(s_upmm if dc == 7 else None), start=(dc == 0), stop=(dc == 7))
            k.wait("scalar", s_upmm, s_upmm.n)
            vge = k.op("scalar", "activation", t["gsg"][qq][:, :], sgp, sem=s_gelu, func=AF.Gelu, bias=t["zeroc"][:, :])
            k.wait("vector", s_gelu, vge)
            pair_free[qq] = k.op("vector", "tensor_tensor", t["hsh"][:, ist, :], t["gsg"][qq][:, :], sup,
                                 sem=s_hsh, op=ALU.mult)
        k.wait("tensor", s_wldd, wld_dn)
        k.wait("tensor", s_hsh, s_hsh.n)
        for ts in ([] if os.environ.get("KSKIP_DOWN") else range(4)):
            tile = sl * 4 + ts
            tpar = tile % 2
            if tpar in x1_st:
                k.wait("sync", s_x1st[tpar], x1_st[tpar])
            vx2 = k.dma("sync", t["xt2"][tpar][:, :], src[tile * 128:(tile + 1) * 128, :], s_x2)
            for nch in range(2):
                rr = (tile * 2 + nch) % 2
                dn = t["PS_B"][:, rr * 512:(rr + 1) * 512]
                if rr in dn_free:
                    k.wait("tensor", s_x1v, dn_free[rr])
                for ic in range(16):
                    k.op("tensor", "matmul", dn,
                         t["hsh"][:, ic, ts * 128:(ts + 1) * 128],
                         t["sdw"][:, ic, nch * 512:(nch + 1) * 512],
                         sem=(s_dnmm if ic == 15 else None), start=(ic == 0), stop=(ic == 15))
                k.wait("vector", s_dnmm, s_dnmm.n)
                k.wait("vector", s_x2, vx2)
                dn_free[rr] = k.op("vector", "scalar_tensor_tensor",
                                   t["xacc"][tpar][:, nch * 512:(nch + 1) * 512], dn, 0.5,
                                   t["xt2"][tpar][:, nch * 512:(nch + 1) * 512],
                                   sem=s_x1v, op0=ALU.mult, op1=ALU.add)
            k.wait("sync", s_x1v, s_x1v.n)
            x1_st[tpar] = k.dma("sync", xmid[tile * 128:(tile + 1) * 128, :], t["xacc"][tpar][:, :], s_x1st[tpar])
        xT_free[spar] = s_upmm.n
    return (s_x1st[0].n, s_x1st[1].n)


def emit_routing(k, app):
    t = k.t
    s_scst = k.sem("scst")
    s_cc = k.sem("cc")
    s_scg = k.sem("scg")
    s_bv = k.sem("bv")
    s_bt = k.sem("bt")
    s_rk = k.sem("rk")

    k.wait("gpsimd", s_scst, s_scst.n)
    s_cc.n += 1
    def cc(E, i=t["sc_loc"], o=t["sc_glob"], s=s_cc.h):
        E.collective_compute("AllGather", ALU.bypass, replica_groups=[list(range(NCORES))],
                             ins=[i.tensor.ap().opt()], outs=[o.tensor.ap().opt()]).then_inc(s, 1)
    k.em("gpsimd", cc)
    k.wait("sync", s_cc, s_cc.n)
    vscg = k.dma("sync", t["scg"][:, :, :], t["sc_glob"].rearrange("(p q) e -> p q e", p=128), s_scg)
    k.wait("vector", s_scg, vscg)
    k.op("vector", "memset", t["theta"][:, :], 0.5)
    step = 0.25
    scv = t["scg"][:, :, :].rearrange("p q e -> p e q")
    thb = t["theta"][:, :].unsqueeze(-1).broadcast_to([128, 16, 256])
    cmpv = t["cmp"][:, :].rearrange("p (e q) -> p e q", e=16)
    for it in range(NBIS):
        k.op("vector", "tensor_tensor", cmpv, scv, thb, op=ALU.is_ge)
        vred = k.op("vector", "tensor_reduce", t["red"][:, :], cmpv, AX.X, ALU.add, sem=s_bv, allow_lp=True)
        k.wait("tensor", s_bv, vred)
        vcnt = k.op("tensor", "matmul", t["PS_C"][:, 16:32], t["c_ones128"][:, :], t["red"][:, :], sem=s_bt,
                    start=True, stop=True)
        k.wait("vector", s_bt, vcnt)
        k.op("vector", "tensor_single_scalar", t["dir"][:, :], t["PS_C"][:, 16:32], 4096.0, op=ALU.is_gt)
        k.op("vector", "scalar_tensor_tensor", t["theta"][:, :], t["dir"][:, :], 2.0 * step, t["theta"][:, :],
             op0=ALU.mult, op1=ALU.add)
        k.op("vector", "tensor_scalar_add", t["theta"][:, :], t["theta"][:, :], -step, sem=s_bv)
        step *= 0.5
    scb = t["scores"][:, :].rearrange("p (tt e) -> p tt e", e=16)
    thb2 = t["theta"][:, :].unsqueeze(1).broadcast_to([128, NT, 16])
    vm = k.op("vector", "tensor_tensor", t["mask"][:, :].rearrange("p (tt e) -> p tt e", e=16), scb, thb2,
              sem=s_bv, op=ALU.is_ge)
    k.wait("tensor", s_bv, vm)
    k.op("tensor", "matmul", t["PS_A"][:, 0:512], t["c_t1"][:, :], t["mask"][:, :], start=True, stop=True)
    for c in range(4):
        k.op("tensor", "matmul", t["PS_C"][:, 32 + c:33 + c], t["mask"][:, c * 128:(c + 1) * 128],
             t["c_ones_col"][:, :], sem=(s_bt if c == 3 else None), start=True, stop=True)
    k.wait("vector", s_bt, s_bt.n)
    vcs = k.op("vector", "tensor_copy", t["cs_bf"][:, :], t["PS_C"][:, 32:36], sem=s_bv)
    k.wait("tensor", s_bv, vcs)
    for c in range(4):
        for cp in range(c + 1):
            k.op("tensor", "matmul", t["PS_C"][:, 36 + c:37 + c],
                 t["c_tdiag"][:, :] if cp == c else t["c_tfull"][:, :],
                 t["cs_bf"][:, cp:cp + 1],
                 sem=(s_bt if (c == 3 and cp == 3) else None), start=(cp == 0), stop=(cp == c))
    k.wait("vector", s_bt, s_bt.n)
    voff = k.op("vector", "tensor_copy", t["off_f"][:, :], t["PS_C"][:, 36:40], sem=s_bv)
    s_rt = k.sem("rowt")
    k.wait("sync", s_bv, voff)
    vst = k.dma("sync", t["rowtmp"].rearrange("(c p) one -> p c one", p=128), t["off_f"][:, :].unsqueeze(-1), s_rt, allow_slow_non_contiguous=True)
    k.wait("sync", s_rt, vst)
    vld = k.dma("sync", t["orow"][0:1, :], t["rowtmp"].rearrange("q one -> one q"), s_rt)
    k.wait("tensor", s_rt, vld)
    vbc = k.op("tensor", "matmul", t["PS_A"][:, 1024:1536], t["ones_row"][0:1, :], t["orow"][0:1, :], sem=s_bt,
               start=True, stop=True)
    k.wait("vector", s_bt, vbc)
    k.op("vector", "tensor_copy", t["prefw"][:, :], t["PS_A"][:, 0:512])
    k.op("vector", "tensor_add", t["rank"][:, :], t["prefw"][:, :], t["PS_A"][:, 1024:1536])
    k.op("vector", "tensor_scalar", t["notm"][:, :], t["mask"][:, :], -1.0, 1.0, op0=ALU.mult, op1=ALU.add)
    k.op("vector", "scalar_tensor_tensor", t["idxf_f"][:, :], t["notm"][:, :], t["c_bigiota"][:, :],
         t["rank"][:, :], op0=ALU.mult, op1=ALU.add)
    k.op("vector", "tensor_copy", t["idxf"][:, :], t["idxf_f"][:, :])
    k.op("vector", "tensor_scalar_min", t["idxb_f"][:, :], t["rank"][:, :], float(CPAD - 1))
    k.op("vector", "tensor_copy", t["idxb"][:, :], t["idxb_f"][:, :])
    k.op("vector", "tensor_single_scalar", t["rok"][:, :], t["rank"][:, :], float(CPAD), op=ALU.is_lt)
    k.op("vector", "tensor_scalar_mul", t["weff"][:, :], t["scores"][:, :], 0.5)
    k.op("vector", "tensor_mul", t["weff"][:, :], t["weff"][:, :], t["mask"][:, :])
    vrk = k.op("vector", "tensor_mul", t["weff"][:, :], t["weff"][:, :], t["rok"][:, :], sem=s_rk)
    return vrk


def emit_fwd_scatter(k, app, rk_val):
    t = k.t
    s_rk = k.sem("rk")
    s_xr = [k.sem("xr0"), k.sem("xr1")]
    s_xrfree = k.sem("xrfree")
    s_fsc = [[k.sem(f"fsc{i}_0"), k.sem(f"fsc{i}_1")] for i in range(4)]
    s_zero = k.sem("zero")
    k.wait("gpsimd", s_rk, rk_val)
    k.wait("gpsimd", s_zero, s_zero.n)
    free_at = {}
    group_done = []
    prev_tile_sc = None
    for eg in range(4):
        for tt in range(NT):
            par = tt % 2
            if par in free_at:
                k.wait("sync", s_xrfree, free_at[par])
            vxr = k.dma("sync", t["xlt"][par][:, :], t["xln"][tt * 128:(tt + 1) * 128, :], s_xr[par])
            k.wait("gpsimd", s_xr[par], vxr)
            fsem = s_fsc[eg][par]
            for ei in range(4):
                e = eg * 4 + ei
                idx = t["idxf"][:, tt * 16 + e: tt * 16 + e + 1]
                fsem.n += 16
                def fsc(E, o=t[f"xg{e}"], ia=t["xlt"][par][:, :], ix=idx, s=fsem.h, _k=k):
                    E.indirect_dma_start(out=o[:, :], out_offset=IndirectOffsetOnAxis(ap=ix, axis=0),
                                         in_=ia, in_offset=None,
                                         bounds_check=_k.bc_reg, oob_is_err=False).then_inc(s, 16)
                k.em("gpsimd", fsc)
            if prev_tile_sc is not None:
                k.wait("gpsimd", prev_tile_sc[0], prev_tile_sc[1])
                free_at[1 - par] = k.sem_inc("gpsimd", s_xrfree)
            prev_tile_sc = (fsem, fsem.n)
        group_done.append([(s_fsc[eg][0], s_fsc[eg][0].n), (s_fsc[eg][1], s_fsc[eg][1].n)])
    return group_done


def emit_experts(k, app, group_done, rk_val, shared_done):
    t = k.t
    s_rk = k.sem("rk")
    s_x1v = k.sem("x1v")
    s_xgT = k.sem("xgT")
    s_xgTfree = k.sem("xgTfree")
    s_wgu = [k.sem("wgu0"), k.sem("wgu1")]
    s_wd = [k.sem("wdq0"), k.sem("wdq1")]
    s_up = k.sem("upF")
    s_gelu2 = k.sem("gelu2")
    s_h2 = k.sem("h2")
    s_dn = k.sem("dnF")
    s_eov = k.sem("eov")
    s_eost = [k.sem("eost0"), k.sem("eost1")]

    k.wait("gpsimd", s_x1v, shared_done)
    k.wait("tensor", s_rk, rk_val)

    wd_free_at = {}
    xgT_free = {}
    pair_free = {}
    dn_all_prev = None
    eo_rot_free = {}
    wgu_done = {}
    wd_done = {}
    eo_store_prev = {}
    eost_marks = []
    for e in range(NE):
        ep = e % 2
        grp = e // 4
        for gsem, gval in group_done[grp]:
            k.wait("sync", gsem, gval)
        if ep in xgT_free:
            k.wait("sync", k.sem("upF"), xgT_free[ep])
        for dc in range(8):
            k.dma("sync", t["xgT"][ep][:, dc, :],
                  t[f"xg{e}"][:, dc * 128:(dc + 1) * 128], s_xgT, transpose=True)
        vxgT = s_xgT.n
        up_base_e = s_up.n
        for ig in range(8):
            gi = e * 8 + ig
            if gi >= 2:
                if ig >= 2:
                    k.wait("gpsimd", s_up, up_base_e + 8 * (ig - 1))
                else:
                    k.wait("gpsimd", s_up, up_base_e - 8 * (1 - ig))
            wp = gi % 2
            k.dma("gpsimd", t["wgb"][wp][:, :, :],
                  t["wgt"][e, :, ig * 512:(ig + 1) * 512].rearrange("(dc p) i -> p dc i", p=128), s_wgu[wp])
            k.dma("gpsimd", t["wub"][wp][:, :, :],
                  t["wut"][e, :, ig * 512:(ig + 1) * 512].rearrange("(dc p) i -> p dc i", p=128), s_wgu[wp])
            wgu_done[(e, ig)] = s_wgu[wp].n
        for hf in range(2):
            wi = e * 2 + hf
            if wi >= 2:
                k.wait("gpsimd", s_dn, wd_free_at[wi - 2])
            hp = wi % 2
            for q in range(4):
                k.dma("gpsimd", t["wdb"][hp][:, q * 8:(q + 1) * 8, :],
                      t["wdt"][e, q * 1024:(q + 1) * 1024, hf * 512:(hf + 1) * 512]
                      .rearrange("(ic p) d -> p ic d", p=128), s_wd[hp])
            wd_done[(e, hf)] = s_wd[hp].n
        k.wait("tensor", s_xgT, vxgT)
        if dn_all_prev is not None:
            k.wait("vector", s_dn, dn_all_prev)
        for ig in range(8):
            gi = e * 8 + ig
            wp = gi % 2
            k.wait("tensor", s_wgu[wp], wgu_done[(e, ig)])
            for it in range(4):
                itile = ig * 4 + it
                qq = itile % 2
                g5 = t["PS_A"][:, qq * 1024: qq * 1024 + 512]
                u5 = t["PS_A"][:, qq * 1024 + 512: qq * 1024 + 1024]
                if qq in pair_free:
                    k.wait("tensor", s_h2, pair_free[qq])
                for dc in range(8):
                    k.op("tensor", "matmul", g5, t["wgb"][wp][:, dc, it * 128:(it + 1) * 128],
                         t["xgT"][ep][:, dc, 0:512], start=(dc == 0), stop=(dc == 7))
                for dc in range(8):
                    k.op("tensor", "matmul", u5, t["wub"][wp][:, dc, it * 128:(it + 1) * 128],
                         t["xgT"][ep][:, dc, 0:512],
                         sem=(s_up if dc == 7 else None), start=(dc == 0), stop=(dc == 7))
                k.wait("scalar", s_up, s_up.n)
                vge = k.op("scalar", "activation", t["gsg2"][qq][:, 0:512], g5, sem=s_gelu2, func=AF.Gelu, bias=t["zeroc"][:, :])
                k.wait("vector", s_gelu2, vge)
                pair_free[qq] = k.op("vector", "tensor_tensor", t["hsb"][:, itile, 0:512],
                                     t["gsg2"][qq][:, 0:512], u5, sem=s_h2, op=ALU.mult)
            for it in range(4):
                itile = ig * 4 + it
                qq = itile % 2
                g6 = t["PS_A"][:, qq * 1024: qq * 1024 + 64]
                u6 = t["PS_A"][:, qq * 1024 + 512: qq * 1024 + 512 + 64]
                k.wait("tensor", s_h2, pair_free[qq])
                for dc in range(8):
                    k.op("tensor", "matmul", g6, t["wgb"][wp][:, dc, it * 128:(it + 1) * 128],
                         t["xgT"][ep][:, dc, 512:576], start=(dc == 0), stop=(dc == 7))
                for dc in range(8):
                    k.op("tensor", "matmul", u6, t["wub"][wp][:, dc, it * 128:(it + 1) * 128],
                         t["xgT"][ep][:, dc, 512:576],
                         sem=(s_up if dc == 7 else None), start=(dc == 0), stop=(dc == 7))
                k.wait("scalar", s_up, s_up.n)
                vge = k.op("scalar", "activation", t["gsg2"][qq][:, 512:576], g6, sem=s_gelu2, func=AF.Gelu, bias=t["zeroc"][:, :])
                k.wait("vector", s_gelu2, vge)
                pair_free[qq] = k.op("vector", "tensor_tensor", t["hsb"][:, itile, 512:576],
                                     t["gsg2"][qq][:, 512:576], u6, sem=s_h2, op=ALU.mult)
        xgT_free[ep] = s_up.n
        k.wait("tensor", s_h2, s_h2.n)
        for hf in range(2):
            wi = e * 2 + hf
            hp = wi % 2
            k.wait("tensor", s_wd[hp], wd_done[(e, hf)])
            for ci, (c0, cw) in enumerate(CSL):
                rr = (wi * 5 + ci) % 2
                dn = t["PS_B"][:, rr * 512:(rr + 1) * 512]
                if rr in eo_rot_free:
                    k.wait("tensor", s_eov, eo_rot_free[rr])
                for ic in range(32):
                    k.op("tensor", "matmul", dn[0:cw, :],
                         t["hsb"][:, ic, c0:c0 + cw],
                         t["wdb"][hp][:, ic, :],
                         sem=(s_dn if ic == 31 else None), start=(ic == 0), stop=(ic == 31))
                k.wait("vector", s_dn, s_dn.n)
                if rr in eo_store_prev:
                    k.wait("vector", s_eost[rr], eo_store_prev[rr])
                veov = k.op("vector", "tensor_copy", t["eo_sb"][rr][0:cw, :], dn[0:cw, :], sem=s_eov)
                eo_rot_free[rr] = veov
                k.wait("sync", s_eov, veov)
                eo_store_prev[rr] = k.dma("sync", t[f"eo{e}"][c0:c0 + cw, hf * 512:(hf + 1) * 512],
                                          t["eo_sb"][rr][0:cw, :], s_eost[rr])
            wd_free_at[wi] = s_dn.n
        eost_marks.append((s_eost[0].n, s_eost[1].n))
        dn_all_prev = s_dn.n
    return eost_marks


def emit_bwd_half(k, app, elist, gate_eost, src, dst, pre_ost=None):
    t = k.t
    s_eost = [k.sem("eost0"), k.sem("eost1")]
    s_x3 = k.sem("x3")
    s_bg = [k.sem(f"bg{i}") for i in range(4)]
    s_fma = k.sem("fma")
    s_ost = [k.sem("ost0"), k.sem("ost1")]
    k.wait("gpsimd", s_eost[0], gate_eost[0])
    k.wait("gpsimd", s_eost[1], gate_eost[1])
    if pre_ost is not None:
        k.wait("sync", s_ost[0], pre_ost[0])
        k.wait("sync", s_ost[1], pre_ost[1])
        k.wait("gpsimd", s_fma, s_fma.n)
    gt_free = {}
    st_at = {}
    gidx = 0
    for tt in range(NT):
        a0 = t["xt2"][tt % 2]
        a1 = t["xacc"][tt % 2]
        if tt % 2 in st_at:
            k.wait("sync", s_ost[tt % 2], st_at[tt % 2])
        vx3 = k.dma("sync", a0[:, :], src[tt * 128:(tt + 1) * 128, :], s_x3)
        k.wait("vector", s_x3, vx3)
        ne = len(elist)
        for ei, e in enumerate(elist):
            gp = gidx % 4
            gidx += 1
            if gp in gt_free:
                k.wait("gpsimd", s_fma, gt_free[gp])
            idx = t["idxb"][:, tt * 16 + e: tt * 16 + e + 1]
            s_bg[gp].n += 16
            def bg(E, o=t["gt"][gp][:, :], ia=t[f"eo{e}"], ix=idx, s=s_bg[gp].h):
                E.indirect_dma_start(out=o, out_offset=None,
                                     in_=ia[:, :], in_offset=IndirectOffsetOnAxis(ap=ix, axis=0)).then_inc(s, 16)
            k.em("gpsimd", bg)
            k.wait("vector", s_bg[gp], s_bg[gp].n)
            src_acc = a0 if ei % 2 == 0 else a1
            dst_acc = a1 if ei % 2 == 0 else a0
            vfma = k.op("vector", "scalar_tensor_tensor", dst_acc[:, :], t["gt"][gp][:, :],
                 t["weff"][:, tt * 16 + e: tt * 16 + e + 1], src_acc[:, :],
                 sem=s_fma, op0=ALU.mult, op1=ALU.add)
            gt_free[gp] = vfma
        k.wait("sync", s_fma, s_fma.n)
        res = a0 if ne % 2 == 0 else a1
        st_at[tt % 2] = k.dma("sync", dst[tt * 128:(tt + 1) * 128, :], res[:, :], s_ost[tt % 2])
    return s_ost


def emit_bwd(k, app, eost_marks, x1st_all, dst):
    t = k.t
    s_x1st = [k.sem("x1st0"), k.sem("x1st1")]
    k.wait("sync", s_x1st[0], x1st_all[0])
    k.wait("sync", s_x1st[1], x1st_all[1])
    s_ost = emit_bwd_half(k, app, list(range(8)), eost_marks[7], t["xmid"], t["xmid"])
    mid = (s_ost[0].n, s_ost[1].n)
    return emit_bwd_half(k, app, list(range(8, 16)), eost_marks[15], t["xmid"], dst, pre_ost=mid)


def emit_attention(k, gate, ln_base):
    t = k.t
    s_aT = k.sem("aT")
    s_aw = [k.sem("aw0"), k.sem("aw1")]
    s_awo = k.sem("awo")
    s_qk = k.sem("qk")
    s_qke = k.sem("qke")
    s_v = k.sem("v")
    s_ve = k.sem("ve")
    s_sc2 = k.sem("sc2")
    s_ex = k.sem("ex")
    s_av = k.sem("av")
    s_oe = k.sem("oe")
    s_sd = k.sem("sd")
    s_rc = k.sem("rc")
    s_ts = k.sem("ts")
    s_fl = k.sem("fl")
    s_wo = k.sem("wo")
    s_woe = k.sem("woe")
    s_ax = k.sem("ax")
    s_axv = k.sem("axv")
    s_axst = [k.sem(f"axst{i}") for i in range(4)]
    s_xlnst = [k.sem("xlnst0"), k.sem("xlnst1")]

    for gs, gv in gate:
        k.wait("gpsimd", gs, gv)
        k.wait("sync", gs, gv)

    n_qkvw = 0
    qkvw_free_at = {}
    woe_after_batch = None
    ax_st = {}
    for b in range(2):
        k.wait("sync", s_xlnst[0], ln_base[0] + 16 * 8 * (b + 1))
        k.wait("sync", s_xlnst[1], ln_base[1] + 16 * 8 * (b + 1))
        if woe_after_batch is not None:
            k.wait("sync", s_woe, woe_after_batch)
        for dc in range(8):
            k.dma("sync", t["axlnT"][:, dc, :],
                  t["xln"][b * S:(b + 1) * S, dc * 128:(dc + 1) * 128], s_aT, transpose=True)
        vaT = s_aT.n
        k.wait("tensor", s_aT, vaT)
        qk_evac = {}
        for mat in range(2):
            dstq = t["qT"] if mat == 0 else t["kT"]
            for hh in range(8):
                wpar = n_qkvw % 2
                if n_qkvw >= 2:
                    fs, fv = qkvw_free_at[n_qkvw - 2]
                    k.wait("gpsimd", fs, fv)
                k.dma("gpsimd", t["qkvw"][wpar][:, :, :],
                      t["qkvwd"][mat, :, hh * 128:(hh + 1) * 128].rearrange("(dc p) f -> p dc f", p=128), s_aw[wpar])
                cur_qkvw = n_qkvw
                n_qkvw += 1
                k.wait("tensor", s_aw[wpar], s_aw[wpar].n)
                for nch in range(4):
                    rr = nch % 2
                    ps = t["PS_B"][:, rr * 512:(rr + 1) * 512]
                    if rr in qk_evac:
                        k.wait("tensor", s_qke, qk_evac[rr])
                    for dc in range(8):
                        k.op("tensor", "matmul", ps, t["qkvw"][wpar][:, dc, :],
                             t["axlnT"][:, dc, nch * 512:(nch + 1) * 512],
                             sem=(s_qk if dc == 7 else None), start=(dc == 0), stop=(dc == 7))
                    k.wait("vector", s_qk, s_qk.n)
                    qk_evac[rr] = k.op("vector", "tensor_copy",
                                       dstq[:, hh, nch * 512:(nch + 1) * 512], ps, sem=s_qke)
                qkvw_free_at[cur_qkvw] = (s_qk, s_qk.n)
        k.op("vector", "memset",
             t["vsb"][:, :, :].rearrange("p a (h o) -> p a h o", h=16)[:, :, :, 64:65], 1.0)
        vev = None
        for hh in range(8):
            wpar = n_qkvw % 2
            fs, fv = qkvw_free_at[n_qkvw - 2]
            k.wait("gpsimd", fs, fv)
            k.dma("gpsimd", t["qkvw"][wpar][:, :, :],
                  t["qkvwd"][2, :, hh * 128:(hh + 1) * 128].rearrange("(dc p) f -> p dc f", p=128), s_aw[wpar])
            cur_qkvw = n_qkvw
            n_qkvw += 1
            k.wait("tensor", s_aw[wpar], s_aw[wpar].n)
            for ts in range(16):
                if vev is not None:
                    k.wait("tensor", s_ve, vev)
                for dc in range(8):
                    k.op("tensor", "matmul", t["PS_C"][:, 0:128],
                         t["axlnT"][:, dc, ts * 128:(ts + 1) * 128],
                         t["qkvw"][wpar][:, dc, :],
                         sem=(s_v if dc == 7 else None), start=(dc == 0), stop=(dc == 7))
                k.wait("vector", s_v, s_v.n)
                k.op("vector", "tensor_copy", t["vsb"][:, ts, (2 * hh) * 65:(2 * hh) * 65 + 64],
                     t["PS_C"][:, 0:64])
                vev = k.op("vector", "tensor_copy", t["vsb"][:, ts, (2 * hh + 1) * 65:(2 * hh + 1) * 65 + 64],
                           t["PS_C"][:, 64:128], sem=s_ve)
            qkvw_free_at[cur_qkvw] = (s_v, s_v.n)
        oT_evac_prev = None
        for h in range(NH):
            hh, hp = h // 2, h % 2
            kbase = hp * 64
            prev_ex = {}
            prev_av = {}
            for tt in range(16):
                for sc in range(2):
                    i2 = tt * 2 + sc
                    if i2 >= 1 and (i2 - 1) in prev_ex:
                        k.wait("tensor", s_ex, prev_ex[i2 - 1])
                    for j in range(2):
                        k.op("tensor", "matmul", t["PS_B"][:, j * 512:(j + 1) * 512],
                             t["kT"][kbase:kbase + 64, hh, tt * 128:(tt + 1) * 128],
                             t["qT"][kbase:kbase + 64, hh, sc * 1024 + j * 512: sc * 1024 + (j + 1) * 512],
                             sem=(s_sc2 if j == 1 else None), start=True, stop=True)
                    k.wait("scalar", s_sc2, s_sc2.n)
                    par = i2 % 2
                    if par in prev_av:
                        k.wait("scalar", s_av, prev_av[par])
                    vex = k.op("scalar", "activation", t["expT"][par][:, :], t["PS_B"][:, 0:1024],
                               sem=s_ex, func=AF.Exp, scale=ATT_SCALE, bias=t["zeroc"][:, :])
                    prev_ex[i2] = vex
                    k.wait("tensor", s_ex, vex)
                    if tt == 0 and oT_evac_prev is not None:
                        k.wait("tensor", s_oe, oT_evac_prev)
                    for j in range(2):
                        k.op("tensor", "matmul",
                             t["PS_A"][0:65, sc * 1024 + j * 512: sc * 1024 + (j + 1) * 512],
                             t["vsb"][:, tt, h * 65:(h + 1) * 65],
                             t["expT"][par][:, j * 512:(j + 1) * 512],
                             sem=(s_av if j == 1 else None), start=(tt == 0), stop=(tt == 15))
                    prev_av[par] = s_av.n
            k.wait("vector", s_av, s_av.n)
            k.op("vector", "tensor_copy", t["oT_sb"][0:64, :], t["PS_A"][0:64, :])
            oT_evac_prev = k.op("vector", "tensor_copy", t["sums"][0:1, :], t["PS_A"][64:65, :], sem=s_oe)
            k.wait("tensor", s_oe, oT_evac_prev)
            for ts in range(16):
                k.op("tensor", "matmul", t["PS_C"][:, ts:ts + 1],
                     t["sums"][0:1, ts * 128:(ts + 1) * 128], t["ones_row"][0:1, 0:1],
                     sem=(s_sd if ts == 15 else None), start=True, stop=True)
            k.wait("vector", s_sd, s_sd.n)
            vrc = k.op("vector", "reciprocal", t["rcol"][:, :], t["PS_C"][:, 0:16], sem=s_rc)
            k.wait("tensor", s_rc, vrc)
            for ts in range(16):
                v1 = k.op("tensor", "transpose", t["PT"][:, 0:64],
                          t["oT_sb"][0:64, ts * 128:(ts + 1) * 128], t["c_ident_bf"][0:64, 0:64], sem=s_ts)
                k.wait("vector", s_ts, v1)
                v2 = k.op("vector", "tensor_scalar_mul", t["otk"][:, :], t["PT"][:, 0:64],
                          t["rcol"][:, ts:ts + 1], sem=s_ts)
                k.wait("tensor", s_ts, v2)
                v3 = k.op("tensor", "transpose", t["PT"][0:64, 64:192], t["otk"][:, :],
                          t["c_ident_bf"][:, :], sem=s_ts)
                k.wait("vector", s_ts, v3)
                v4 = k.op("vector", "tensor_copy",
                          t["flatT"][kbase:kbase + 64, hh, ts * 128:(ts + 1) * 128],
                          t["PT"][0:64, 64:192], sem=s_fl)
                k.wait("tensor", s_fl, v4)
        k.wait("gpsimd", s_fl, s_fl.n)
        if woe_after_batch is not None:
            k.wait("gpsimd", s_woe, woe_after_batch)
        for fq in range(4):
            vwo = k.dma("gpsimd", t["wo_sb"][:, fq * 2:(fq + 1) * 2, :],
                        t["wod"][fq * 256:(fq + 1) * 256, :].rearrange("(fc p) m -> p fc m", p=128), s_awo)
        k.wait("tensor", s_awo, vwo)
        for scq in range(4):
            for j4 in range(4):
                stile = scq * 4 + j4
                if j4 in ax_st:
                    k.wait("sync", s_axst[j4], ax_st[j4])
                k.dma("sync", t["axc"][j4][:, :],
                      t["xmid"][(b * 16 + stile) * 128:(b * 16 + stile + 1) * 128, :], s_ax)
            vax = s_ax.n
            k.wait("vector", s_ax, vax)
            woe_prev = {}
            for mt in range(8):
                rr = mt % 2
                ps = t["PS_B"][:, rr * 512:(rr + 1) * 512]
                if rr in woe_prev:
                    k.wait("tensor", s_woe, woe_prev[rr])
                for fc in range(8):
                    k.op("tensor", "matmul", ps, t["wo_sb"][:, fc, mt * 128:(mt + 1) * 128],
                         t["flatT"][:, fc, scq * 512:(scq + 1) * 512],
                         sem=(s_wo if fc == 7 else None), start=(fc == 0), stop=(fc == 7))
                k.wait("vector", s_wo, s_wo.n)
                vwoe = k.op("vector", "tensor_copy", t["wotmp"][rr][:, :], ps, sem=s_woe)
                woe_prev[rr] = vwoe
                k.wait("tensor", s_woe, vwoe)
                for j4 in range(4):
                    pslot = 192 + (j4 % 2) * 128
                    vt = k.op("tensor", "transpose", t["PT"][:, pslot:pslot + 128],
                              t["wotmp"][rr][:, j4 * 128:(j4 + 1) * 128], t["c_ident_bf"][:, :], sem=s_ts)
                    k.wait("vector", s_ts, vt)
                    vv = k.op("vector", "tensor_add", t["axc"][j4][:, mt * 128:(mt + 1) * 128],
                              t["axc"][j4][:, mt * 128:(mt + 1) * 128],
                              t["PT"][:, pslot:pslot + 128], sem=s_axv)
                    k.wait("tensor", s_axv, vv)
            k.wait("sync", s_axv, s_axv.n)
            for j4 in range(4):
                stile = scq * 4 + j4
                ax_st[j4] = k.dma("sync", t["xmid"][(b * 16 + stile) * 128:(b * 16 + stile + 1) * 128, :],
                                  t["axc"][j4][:, :], s_axst[j4])
        woe_after_batch = s_woe.n
    return s_axst


def build(stage="full"):
    nc = bass.Bass("TRN2", target_bir_lowering=False)
    ctx = ExitStack()
    k = K(nc, ctx)
    t = k.t
    sa = SA(nc)

    def din(name, shape, dtype=F32):
        t[name] = nc.dram_tensor(name, list(shape), dtype, kind="ExternalInput").ap()
    din("x_in", [TLOC, D])
    din("qkvwd", [3, D, D])
    din("wod", [D, D])
    din("gatewd", [D, 16])
    din("wgt", [NE, D, 4096])
    din("wut", [NE, D, 4096])
    din("wdt", [NE, 4096, D])
    din("sgt", [D, 2048])
    din("sut", [D, 2048])
    din("sdt", [2048, D])
    din("c_t1d", [128, 128], BF16)
    din("c_tfulld", [128, 128], BF16)
    din("c_tdiagd", [128, 128], BF16)
    din("c_identbd", [128, 128], BF16)
    din("c_identfd", [128, 128], F32)
    din("c_ones128d", [128, 128], BF16)
    din("c_onescold", [128, 1], BF16)
    din("c_bigiotad", [128, 1], F32)
    t["y"] = nc.dram_tensor("y", [TLOC, D], F32, kind="ExternalOutput").ap()
    t["xmid"] = nc.dram_tensor("xmid", [TLOC, D], F32).ap()
    t["xln"] = nc.dram_tensor("xln", [TLOC, D], BF16).ap()
    t["sc_loc"] = nc.dram_tensor("sc_loc", [TLOC, 16], F32).ap()
    t["sc_glob"] = nc.dram_tensor("sc_glob", [TLOC * NCORES, 16], F32).ap()
    t["rowtmp"] = nc.dram_tensor("rowtmp", [512, 1], F32).ap()
    for e in range(NE):
        t[f"xg{e}"] = nc.dram_tensor(f"xg{e}", [CPAD, D], BF16).ap()
        t[f"eo{e}"] = nc.dram_tensor(f"eo{e}", [CPAD, D], BF16).ap()

    xt_off = sa.off
    t["xt"] = [sa.pers(f"xt{i}", [128, 1024], F32) for i in range(2)]
    xlt_off = sa.off
    t["xlt"] = [sa.pers(f"xlt{i}", [128, 1024], BF16) for i in range(2)]
    t["xt2"] = [sa.pers(f"xt2_{i}", [128, 1024], F32) for i in range(2)]
    t["xacc"] = [sa.pers(f"xacc{i}", [128, 1024], F32) for i in range(2)]
    t["bst"] = sa.pers("bst", [128, 12], F32)
    t["mv"] = sa.pers("mv", [128, 2], F32)
    t["stdv"] = sa.pers("stdv", [128, 1], F32)
    t["rstd"] = sa.pers("rstd", [128, 1], F32)
    t["nmr"] = sa.pers("nmr", [128, 1], F32)
    t["gw"] = sa.pers("gw", [128, 8, 16], BF16)
    t["gmax"] = sa.pers("gmax", [128, 1], F32)
    t["gneg"] = sa.pers("gneg", [128, 1], F32)
    t["gexp"] = sa.pers("gexp", [128, 16], F32)
    t["gsum"] = sa.pers("gsum", [128, 1], F32)
    t["grec"] = sa.pers("grec", [128, 1], F32)
    t["scores"] = sa.pers("scores", [128, 512], F32)
    t["theta"] = sa.pers("theta", [128, 16], F32)
    t["red"] = sa.pers("red", [128, 16], BF16)
    t["dir"] = sa.pers("dir", [128, 16], F32)
    t["cs_bf"] = sa.pers("cs_bf", [128, 4], BF16)
    t["off_f"] = sa.pers("off_f", [128, 4], F32)
    t["orow"] = sa.pers("orow", [1, 512], F32)
    t["ones_row"] = sa.pers("ones_row", [1, 128], F32)
    t["mask"] = sa.pers("mask", [128, 512], BF16)
    t["rank"] = sa.pers("rank", [128, 512], F32)
    t["idxf"] = sa.pers("idxf", [128, 512], U32)
    t["idxb"] = sa.pers("idxb", [128, 512], U32)
    t["weff"] = sa.pers("weff", [128, 512], F32)
    t["gsg"] = [sa.pers(f"gsg{i}", [128, 512], BF16) for i in range(2)]
    t["gsg2"] = [sa.pers(f"gsg2_{i}", [128, 576], BF16) for i in range(2)]
    t["eo_sb"] = [sa.pers(f"eo_sb{i}", [128, 512], BF16) for i in range(2)]
    t["c_t1"] = sa.pers("c_t1", [128, 128], BF16)
    t["c_tfull"] = sa.pers("c_tfull", [128, 128], BF16)
    t["c_tdiag"] = sa.pers("c_tdiag", [128, 128], BF16)
    t["c_ident_bf"] = sa.pers("c_ident_bf", [128, 128], BF16)
    t["c_ident_f32"] = sa.pers("c_ident_f32", [128, 128], F32)
    t["c_ones128"] = sa.pers("c_ones128", [128, 128], BF16)
    t["c_ones_col"] = sa.pers("c_ones_col", [128, 1], BF16)
    t["c_bigiota"] = sa.pers("c_bigiota", [128, 1], F32)
    t["epsc"] = sa.pers("epsc", [128, 1], F32)
    t["zrow"] = sa.pers("zrow", [128, 1024], BF16)
    t["zeroc"] = sa.pers("zeroc", [128, 1], F32)
    sa.close()

    u = 0
    t["sgw"] = sa.union("sgw", [128, 8, 2048], BF16, u); u += 32768
    t["suw"] = sa.union("suw", [128, 8, 2048], BF16, u); u += 32768
    t["sdw"] = sa.union("sdw", [128, 16, 1024], BF16, u); u += 32768
    t["hsh"] = sa.union("hsh", [128, 16, 512], BF16, u); u += 16384
    t["xlnT"] = []
    for i in range(2):
        t["xlnT"].append(sa.union(f"xlnT{i}", [128, 8, 512], BF16, u)); u += 8192
    u = 0
    t["scg"] = sa.union("scg", [128, 256, 16], F32, 0)
    t["cmp"] = sa.union("cmp", [128, 4096], BF16, 16384)
    t["prefw"] = sa.union("prefw", [128, 512], F32, 32768)
    t["notm"] = sa.union("notm", [128, 512], F32, 34816)
    t["idxf_f"] = sa.union("idxf_f", [128, 512], F32, 36864)
    t["idxb_f"] = sa.union("idxb_f", [128, 512], F32, 38912)
    t["rok"] = sa.union("rok", [128, 512], F32, 40960)
    t["gt"] = [nc.alloc_sbuf_tensor_at(f"gt{i}", [128, 1024], BF16, offset=off).ap()
               for i, off in enumerate([xlt_off, xlt_off + 2048, xt_off, xt_off + 4096])]
    t["wgb"] = []
    t["wub"] = []
    for i in range(2):
        t["wgb"].append(sa.union(f"wgb{i}", [128, 8, 512], BF16, u)); u += 8192
    for i in range(2):
        t["wub"].append(sa.union(f"wub{i}", [128, 8, 512], BF16, u)); u += 8192
    t["wdb"] = []
    for i in range(2):
        t["wdb"].append(sa.union(f"wdb{i}", [128, 32, 512], BF16, u)); u += 32768
    t["hsb"] = sa.union("hsb", [128, 32, 576], BF16, u); u += 36864
    t["xgT"] = []
    for i in range(2):
        t["xgT"].append(sa.union(f"xgT{i}", [128, 8, 576], BF16, u)); u += 9216
    u = 0
    t["axlnT"] = sa.union("axlnT", [128, 8, 2048], BF16, u)
    t["flatT"] = sa.union("flatT", [128, 8, 2048], BF16, u); u += 32768
    t["qT"] = sa.union("qT", [128, 8, 2048], BF16, u); u += 32768
    u_kT = u
    t["kT"] = sa.union("kT", [128, 8, 2048], BF16, u); u += 32768
    t["vsb"] = sa.union("vsb", [128, 16, 1040], BF16, u); u += 33280
    t["qkvw"] = []
    for i in range(2):
        t["qkvw"].append(sa.union(f"qkvw{i}", [128, 8, 128], BF16, u)); u += 2048
    t["expT"] = []
    for i in range(2):
        t["expT"].append(sa.union(f"expT{i}", [128, 1024], BF16, u)); u += 2048
    t["oT_sb"] = sa.union("oT_sb", [128, 2048], BF16, u); u += 4096
    t["sums"] = sa.union("sums", [1, 2048], F32, u); u += 8192
    t["rcol"] = sa.union("rcol", [128, 16], F32, u); u += 64
    t["otk"] = sa.union("otk", [128, 64], BF16, u); u += 128
    t["wotmp"] = []
    for i in range(2):
        t["wotmp"].append(sa.union(f"wotmp{i}", [128, 512], BF16, u)); u += 1024
    t["axc"] = [t["xt2"][0], t["xt2"][1], t["xacc"][0], t["xacc"][1]]
    t["wo_sb"] = sa.union("wo_sb", [128, 8, 1024], BF16, u_kT)

    t["PS_A"] = ctx.enter_context(nc.psum_tensor("PS_A", [128, 2048], F32)).ap()
    t["PS_B"] = ctx.enter_context(nc.psum_tensor("PS_B", [128, 1024], F32)).ap()
    t["PS_C"] = ctx.enter_context(nc.psum_tensor("PS_C", [128, 512], F32)).ap()
    t["PT"] = ctx.enter_context(nc.psum_tensor("PT", [128, 448], BF16)).ap()

    def _bc_init(E, _k=k):
        _k.bc_reg = E.alloc_register("bcreg")
        E.reg_mov(_k.bc_reg, CPAD - 1)
    k.em("gpsimd", _bc_init)
    s_init = k.sem("init")
    s_initg = k.sem("initg")
    s_zmem = k.sem("zmem")
    s_zero = k.sem("zero")
    k.dma("gpsimd", t["gw"][:, :, :], t["gatewd"].rearrange("(dc p) e -> p dc e", p=128), s_initg)
    for nm, src in [("c_t1", "c_t1d"), ("c_tfull", "c_tfulld"), ("c_tdiag", "c_tdiagd"),
                    ("c_ident_bf", "c_identbd"), ("c_ident_f32", "c_identfd"),
                    ("c_ones128", "c_ones128d"), ("c_ones_col", "c_onescold"),
                    ("c_bigiota", "c_bigiotad")]:
        k.dma("scalar", t[nm][:, :], t[src], s_init)
    k.op("vector", "memset", t["ones_row"][0:1, :], 1.0)
    k.op("vector", "memset", t["epsc"][:, :], 1e-5)
    k.op("vector", "memset", t["zeroc"][:, :], 0.0)
    vz = k.op("vector", "memset", t["zrow"][:, :], 0.0, sem=s_zmem)
    k.wait("scalar", s_zmem, vz)
    for e in range(NE):
        for r0, rw in CSL:
            k.dma("scalar", t[f"xg{e}"][r0:r0 + rw, :], t["zrow"][0:rw, :], s_zero)
    init_all = s_init.n
    for e in ENGINES:
        k.wait(e, s_init, init_all)
        k.wait(e, s_initg, s_initg.n)
    k.wait("vector", s_zero, s_zero.n)

    def ffn(app, src, dst, gate):
        _, lnb = emit_ln(k, src, f"{app}f", pre=gate and ("sync", gate[0], gate[1]))
        x1_all = emit_gate_shared(k, app, src, t["xmid"], gate, lnb)
        rk_val = emit_routing(k, app)
        group_done = emit_fwd_scatter(k, app, rk_val)
        shared_done = k.sem("x1v").n
        eost_all = emit_experts(k, app, group_done, rk_val, shared_done)
        return emit_bwd(k, app, eost_all, x1_all, dst)

    def ost_pairs(s_o):
        return [(s_o[0], s_o[0].n), (s_o[1], s_o[1].n)]

    if stage.startswith("dbg_"):
        which = stage[4:]
        s_d = k.sem("dbg")
        if which == "ln":
            sem, lnb = emit_ln(k, t["x_in"], "0f")
            k.wait("gpsimd", sem[0], sem[0].n)
            k.wait("gpsimd", sem[1], sem[1].n)
            k.dma("gpsimd", t["y"][:, :], t["xln"][:, :], s_d)
        elif which in ("shared", "shared2", "routing", "scatter", "experts"):
            _, lnb = emit_ln(k, t["x_in"], "0f")
            x1_all = emit_gate_shared(k, 0, t["x_in"], t["xmid"], None, lnb)
            if which == "shared":
                k.wait("gpsimd", k.sem("x1st0"), x1_all[0])
                k.wait("gpsimd", k.sem("x1st1"), x1_all[1])
                k.dma("gpsimd", t["y"][:, :], t["xmid"][:, :], s_d)
            elif which == "shared2":
                k.wait("gpsimd", k.sem("x1st0"), x1_all[0])
                k.wait("gpsimd", k.sem("x1st1"), x1_all[1])
                k.wait("gpsimd", k.sem("hsh"), k.sem("hsh").n)
                k.dma("gpsimd", t["y"][0:128, :], t["hsh"][:, 0:2, :], s_d)
                k.dma("gpsimd", t["y"][128:256, :], t["xlnT"][1][:, 0:2, :], s_d)
                k.dma("gpsimd", t["y"][256:384, :], t["sgw"][:, 0:1, 0:1024], s_d)
                k.dma("gpsimd", t["y"][384:512, :], t["sdw"][:, 0:1, :], s_d)
                k.dma("gpsimd", t["y"][512:640, 0:512], t["gsg"][1][:, :], s_d)
                k.dma("gpsimd", t["y"][640:768, 0:512], t["gsg"][0][:, :], s_d)
                k.dma("gpsimd", t["y"][768:896, 0:512], t["hsh"][:, 15, :], s_d)
                k.dma("gpsimd", t["y"][896:1024, 0:512], t["hsh"][:, 14, :], s_d)
            else:
                rk_val = emit_routing(k, 0)
                if which == "routing":
                    k.wait("gpsimd", k.sem("rk"), rk_val)
                    k.wait("gpsimd", k.sem("x1st0"), x1_all[0])
                    k.wait("gpsimd", k.sem("x1st1"), x1_all[1])
                    k.dma("gpsimd", t["y"][0:128, 0:512].rearrange("p f -> p f"), t["idxf_f"][:, :], s_d)
                    k.dma("gpsimd", t["y"][128:256, 0:512], t["weff"][:, :], s_d)
                    k.dma("gpsimd", t["y"][256:384, 0:512], t["scores"][:, :], s_d)
                    k.dma("gpsimd", t["y"][384:512, 0:16], t["theta"][:, :], s_d)
                    k.dma("gpsimd", t["y"][512:640, 0:512], t["rank"][:, :], s_d)
                else:
                    group_done = emit_fwd_scatter(k, 0, rk_val)
                    if which == "scatter":
                        for grp in group_done:
                            for gsem, gval in grp:
                                k.wait("gpsimd", gsem, gval)
                        k.wait("gpsimd", k.sem("x1st0"), x1_all[0])
                        k.wait("gpsimd", k.sem("x1st1"), x1_all[1])
                        for e in range(7):
                            k.dma("gpsimd", t["y"][e * 576:(e + 1) * 576, :], t[f"xg{e}"][:, :], s_d)
                    else:
                        shared_done = k.sem("x1v").n
                        eost_all = emit_experts(k, 0, group_done, rk_val, shared_done)
                        k.wait("gpsimd", k.sem("eost0"), eost_all[0])
                        k.wait("gpsimd", k.sem("eost1"), eost_all[1])
                        k.wait("gpsimd", k.sem("x1st0"), x1_all[0])
                        k.wait("gpsimd", k.sem("x1st1"), x1_all[1])
                        for e in range(7):
                            k.dma("gpsimd", t["y"][e * 576:(e + 1) * 576, :], t[f"eo{e}"][:, :], s_d)
        k.wait("gpsimd", s_d, s_d.n)
        k.wait("sync", s_d, s_d.n)
    elif stage == "ffn0":
        s_o = ffn(0, t["x_in"], t["y"], None)
        for sem, val in ost_pairs(s_o):
            k.wait("gpsimd", sem, val)
            k.wait("sync", sem, val)
    else:
        s_o0 = ffn(0, t["x_in"], t["xmid"], None)
        o0p = ost_pairs(s_o0)
        _, lnbA = emit_ln(k, t["xmid"], "A", pre=o0p)
        s_att = emit_attention(k, o0p, lnbA)
        attp = [(s, s.n) for s in s_att]
        if stage == "attn":
            s_cp = k.sem("cpout")
            for sem, val in attp:
                k.wait("sync", sem, val)
            for tt in range(NT):
                k.dma("sync", t["y"][tt * 128:(tt + 1) * 128, :], t["xmid"][tt * 128:(tt + 1) * 128, :], s_cp)
            k.wait("gpsimd", s_cp, s_cp.n)
            k.wait("sync", s_cp, s_cp.n)
        else:
            _, lnb1 = emit_ln(k, t["xmid"], "1f", pre=attp)
            x1_all = emit_gate_shared(k, 1, t["xmid"], t["xmid"], attp[0], lnb1)
            for sem, val in attp[1:]:
                k.wait("gpsimd", sem, val)
            rk_val = emit_routing(k, 1)
            group_done = emit_fwd_scatter(k, 1, rk_val)
            shared_done = k.sem("x1v").n
            eost_all = emit_experts(k, 1, group_done, rk_val, shared_done)
            s_o = emit_bwd(k, 1, eost_all, x1_all, t["y"])
            for sem, val in ost_pairs(s_o):
                k.wait("gpsimd", sem, val)
                k.wait("sync", sem, val)

    with nc.Block() as block:
        @block.sync
        def _(E):
            for f in k.streams["sync"]:
                f(E)
        @block.scalar
        def _(E):
            for f in k.streams["scalar"]:
                f(E)
        @block.tensor
        def _(E):
            for f in k.streams["tensor"]:
                f(E)
        @block.vector
        def _(E):
            for f in k.streams["vector"]:
                f(E)
        @block.gpsimd
        def _(E):
            for f in k.streams["gpsimd"]:
                f(E)
    actual = {}
    for bb in nc.main_func.blocks:
        for ins in bb.instructions:
            si = getattr(ins, "sync_info", None)
            if si is not None and si.on_update:
                for u in si.on_update:
                    m = str(u)
                    import re as _re
                    mm_ = _re.search(r"id=(\d+).*?update_value=(\d+)", m)
                    if mm_:
                        sid, val = int(mm_.group(1)), int(mm_.group(2))
                        actual[sid] = actual.get(sid, 0) + val
    for name, sem in k.sems.items():
        sid = sem.h.num
        if sem.n != actual.get(sid, 0):
            raise AssertionError(f"sem {name} (id {sid}): counter {sem.n} != actual {actual.get(sid, 0)}")
    return nc, k


def host_inputs(inputs):
    bf = ml_dtypes.bfloat16
    kern = np.ascontiguousarray(np.asarray(inputs["L0_attn_kernel"]).transpose(0, 2, 1, 3).reshape(3, 1024, 1024))
    wo = np.ascontiguousarray(np.asarray(inputs["L0_attn_WO"])[0].reshape(64, 16, 1024).transpose(1, 0, 2).reshape(1024, 1024))
    gw = np.ascontiguousarray(np.asarray(inputs["L0_ffn_gate_w"]).T)
    wgt = np.ascontiguousarray(np.asarray(inputs["L0_ffn_Wg"]).transpose(0, 2, 1))
    wut = np.ascontiguousarray(np.asarray(inputs["L0_ffn_Wu"]).transpose(0, 2, 1))
    wdt = np.ascontiguousarray(np.asarray(inputs["L0_ffn_Wd"]).transpose(0, 2, 1))
    sgt = np.ascontiguousarray(np.asarray(inputs["L0_ffn_Sg"]).T)
    sut = np.ascontiguousarray(np.asarray(inputs["L0_ffn_Su"]).T)
    sdt = np.ascontiguousarray(np.asarray(inputs["L0_ffn_Sd"]).T)
    i128 = np.arange(128)
    c_t1 = (i128[:, None] < i128[None, :]).astype(bf)
    ke = i128 % 16
    kt = i128 // 16
    c_tfull = (ke[:, None] == ke[None, :]).astype(bf)
    c_tdiag = ((ke[:, None] == ke[None, :]) & (kt[:, None] < kt[None, :])).astype(bf)
    common = dict(
        qkvwd=kern, wod=wo, gatewd=gw, wgt=wgt, wut=wut, wdt=wdt,
        sgt=sgt, sut=sut, sdt=sdt,
        c_t1d=c_t1, c_tfulld=c_tfull, c_tdiagd=c_tdiag,
        c_identbd=np.eye(128, dtype=bf), c_identfd=np.eye(128, dtype=np.float32),
        c_ones128d=np.ones((128, 128), dtype=bf), c_onescold=np.ones((128, 1), dtype=bf),
        c_bigiotad=(BIGIOTA + i128[:, None]).astype(np.float32))
    x = np.asarray(inputs["L0_attn_x"], dtype=np.float32).reshape(8, TLOC, D)
    return [dict(common, x_in=np.ascontiguousarray(x[c])) for c in range(NCORES)]


def kernel(**inputs):
    maps = host_inputs(inputs)
    nc, _ = build("full")
    res = run_bass_kernel_spmd(nc, maps, core_ids=list(range(NCORES)))
    out = np.concatenate([r["y"].reshape(2, 2048, 1024) for r in res.results], axis=0)
    return out.astype(np.float32)
